# version 03
# baseline (speedup 1.0000x reference)
import math
import numpy as np
from contextlib import ExitStack

import concourse.bass as bass
import concourse.mybir as mybir
from concourse.bass_utils import run_bass_kernel_spmd
from concourse.aot_env import allow_ici_all_to_all

F32 = mybir.dt.float32
I32 = mybir.dt.int32
AF = mybir.ActivationFunctionType
ALU = mybir.AluOpType
AXX = mybir.AxisListType.X

N_SUB = 6
NCORE = 8
D = 4096
KC = 32
TPC = 1024
NT = 8
SEQ = 4096
TB = 32
EPS = 1e-5
INTER = 11008
IC = 86
NEG0 = -3.4028235e38
MASK1 = -66504.0
L1_SCALE = 1.0 / math.sqrt(128.0)
L0_SCALE = 1.0 / 128.0

def core_tiles(c):
    return [(0, c + 8 * j) for j in range(4)] + [(1, (7 - c) + 8 * j) for j in range(4)]

GATHER = [core_tiles(r) for r in range(NCORE)]
OWNER = {}
for _r in range(NCORE):
    for _l, _bi in enumerate(GATHER[_r]):
        OWNER[_bi] = (_r, _l)

def gcol(b, i):
    r, l = OWNER[(b, i)]
    return r * TPC + l * 128

def gtile(b, i):
    r, l = OWNER[(b, i)]
    return r * NT + l

def _l0_mask():
    BLOCK, LOCAL, VERT = 64, 16, 8
    nb = SEQ // BLOCK
    qb = np.arange(nb)[:, None]; kb = np.arange(nb)[None, :]
    bm = (kb <= qb) & ((qb - kb < LOCAL) | ((kb + 1) % VERT == 0))
    em = np.kron(bm, np.ones((BLOCK, BLOCK), dtype=bool))
    ii = np.arange(SEQ)
    em &= ii[:, None] >= ii[None, :]
    att, pats, pidx = {}, [], {}
    for a in range(TB):
        lst = []
        for b in range(TB):
            t = em[a * 128:(a + 1) * 128, b * 128:(b + 1) * 128]
            if not t.any():
                continue
            if t.all():
                lst.append((b, -1))
            else:
                key = t.tobytes()
                if key not in pidx:
                    pidx[key] = len(pats)
                    pats.append(np.where(t, 0.0, NEG0).astype(np.float32))
                lst.append((b, pidx[key]))
        att[a] = lst
    return att, np.stack(pats)

L0_ATT, L0_PATS = _l0_mask()
NPAT = len(L0_PATS)

def _rope_tables(base):
    inv = base ** (-(np.arange(64, dtype=np.float64) / 64.0))
    pos = np.zeros(NCORE * TPC, dtype=np.float64)
    for r in range(NCORE):
        for l, (b, i) in enumerate(GATHER[r]):
            pos[r * TPC + l * 128:r * TPC + (l + 1) * 128] = np.arange(i * 128, (i + 1) * 128)
    ang = pos[None, :] * inv[:, None]
    cos = np.cos(ang).astype(np.float32)
    sin = np.sin(ang).astype(np.float32)
    return np.concatenate([cos, cos], 0), np.concatenate([-sin, sin], 0)

def _tile_w(w):
    K, N = w.shape
    kc, nch = K // 128, N // 128
    t = w.reshape(kc, 128, nch, 128).transpose(2, 1, 0, 3).reshape(nch, 128, K)
    return np.ascontiguousarray(t)

ENGS = ("sync", "tensor", "vector", "scalar", "gpsimd")
QSEMS = {"qw": "sync", "qx": "sync", "qo": "sync", "qa": "scalar", "qg": "gpsimd"}

class Prog:
    def __init__(self):
        self.st = {e: [] for e in ENGS}
        self.cnt = {}
        self.waited = {e: {} for e in ENGS}
        self.sems = {}
        self.guard = {}
        self.wmax = {}

    def emit(self, eng, fn):
        self.st[eng].append(fn)

    def inc(self, s, amt):
        self.cnt[s] = self.cnt.get(s, 0) + amt
        return self.cnt[s]

    def wait(self, eng, s, val):
        if not val:
            return
        if s in QSEMS:
            val = self.cnt.get(s, 0)
            if not val:
                return
        self.wmax[s] = max(self.wmax.get(s, 0), val)
        w = self.waited[eng]
        if w.get(s, 0) >= val:
            return
        w[s] = val
        sem = self.sems[s]
        self.emit(eng, lambda e, sem=sem, v=val: e.wait_ge(sem, v))

    def dma(self, q, out, in_, **kw):
        eng = QSEMS[q]
        w = self.wmax.get(q, 0)
        if w:
            ww = self.waited[eng]
            if ww.get(q, 0) < w:
                ww[q] = w
                sem0 = self.sems[q]
                self.emit(eng, lambda e, sem=sem0, v=w: e.wait_ge(sem, v))
        tgt = self.inc(q, 16)
        sem = self.sems[q]
        self.emit(eng, lambda e, o=out, i=in_, kw=kw, sem=sem:
                  e.dma_start(out=o, in_=i, **kw).then_inc(sem, 16))
        return tgt

    ESEM = {"vector": "dv", "scalar": "ac", "tensor": "pe"}

    def op(self, eng, fn, sem=None):
        if eng in ("vector", "scalar"):
            own = self.ESEM[eng]
            if sem is None:
                sem = own
            prev = self.cnt.get(own, 0)
            if prev:
                w = self.waited[eng]
                if w.get(own, 0) < prev:
                    w[own] = prev
                    s0 = self.sems[own]
                    self.emit(eng, lambda e, s0=s0, v=prev: e.wait_ge(s0, v))
        if sem is not None:
            tgt = self.inc(sem, 1)
            s = self.sems[sem]
            self.emit(eng, lambda e, fn=fn, s=s: fn(e).then_inc(s, 1))
            return tgt
        self.emit(eng, lambda e, fn=fn: fn(e))
        return None

    def claim(self, key, eng):
        for s, c in self.guard.get(key, []):
            self.wait(eng, s, c)
        self.guard[key] = []

    def claim_multi(self, key, engs):
        for s, c in self.guard.get(key, []):
            for eng in engs:
                self.wait(eng, s, c)
        self.guard[key] = []

    def consume(self, key, s, c):
        self.guard.setdefault(key, []).append((s, c))


def build_program():
    nc = bass.Bass("TRN2", target_bir_lowering=False)
    P = Prog()

    def din(name, shape, dt=F32):
        return nc.dram_tensor(name, shape, dt, kind="ExternalInput").ap()

    x0 = din("x0", [NT, 128, D])
    eye_in = din("eye", [128, 128])
    r0cos = din("r0cos", [128, NCORE * TPC]); r0sin = din("r0sin", [128, NCORE * TPC])
    r1cos = din("r1cos", [128, NCORE * TPC]); r1sin = din("r1sin", [128, NCORE * TPC])
    m0 = din("m0", [NPAT, 128, 128])
    m1 = din("m1", [128, 128])
    l0k = din("l0k", [KC, 128, D])
    l0kb = din("l0kb", [128, 32])
    gT = din("gT", [16, 8])
    lAT = din("lAT", [16, D, 32])
    lBT = din("lBT", [16, 32, D])
    lAsT = din("lAsT", [KC, 128, 32])
    lBsT = din("lBsT", [32, D])
    w0qkv = din("w0qkv", [6, 128, D])
    b0qkv = din("b0qkv", [128, 6])
    w0d = din("w0d", [KC, 128, 512])
    b0d = din("b0d", [128, 32])
    w1qkv = din("w1qkv", [12, 128, D])
    b1qkv = din("b1qkv", [128, 12])
    w1o = din("w1o", [KC, 128, 512])
    b1o = din("b1o", [128, 32])
    qwg = din("qwg", [D, 2752], I32)
    qzg = din("qzg", [32, 2752], I32)
    scg = din("scg", [32, 22016])
    qwd = din("qwd", [INTER, 512], I32)
    qzd = din("qzd", [IC, 512], I32)
    scd = din("scd", [IC, D])
    out = nc.dram_tensor("out", [NT, 128, D], F32, kind="ExternalOutput").ap()

    def dsc(name, shape, dt=F32):
        return nc.dram_tensor(name, shape, dt).ap()

    xres = dsc("xres", [NT, 128, D])
    aeffT = dsc("aeffT", [8, KC, 128, 32])
    beffT = dsc("beffT", [8, 32, D])
    zdq_g = dsc("zdq_g", [32, 22016])
    zdq_d = dsc("zdq_d", [IC, D])
    wgu_dqA = dsc("wgu_dqA", [86, 128, D])
    wgu_dqB = dsc("wgu_dqB", [86, 128, D])
    wdn_dq = dsc("wdn_dq", [KC, 128, IC * 128])

    def wgu(nch):
        return (wgu_dqA, nch) if nch < 86 else (wgu_dqB, nch - 86)

    def wdn(nch):
        return (wdn_dq, nch)
    cc_in = dsc("cc_in", [KC, 128, TPC])
    ag_out = dsc("ag_out", [NCORE, KC, 128, TPC])
    qkvT = dsc("qkvT", [12, 128, NCORE * TPC])
    vT_tm = dsc("vT_tm", [4, NCORE * NT, 128, 128])
    ctxT_sc = dsc("ctxT_sc", [4, 128, NCORE * TPC])
    rs_in = dsc("rs_in", [NCORE, KC, 128, TPC])
    rs_out = dsc("rs_out", [KC, 128, TPC])

    with ExitStack() as ctx:
        E = ctx.enter_context
        XNT = E(nc.sbuf_tensor("XNT", [128, KC * TPC], F32))
        WB = E(nc.sbuf_tensor("WB", [128, 2, D], F32))
        SC = E(nc.sbuf_tensor("SC", [128, 8192], F32))
        EYE = E(nc.sbuf_tensor("EYE", [128, 128], F32))
        BIAS = E(nc.sbuf_tensor("BIAS", [128, 128], F32))
        STAT = E(nc.sbuf_tensor("STAT", [128, 24], F32))
        MSK = E(nc.sbuf_tensor("MSK", [128, (NPAT + 1) * 128], F32))
        BEF = E(nc.sbuf_tensor("BEF", [32, 2, 9 * 128], F32))
        AEF = E(nc.sbuf_tensor("AEF", [128, 8, 32], F32))
        PTB = E(nc.sbuf_tensor("PTB", [128, 4, 128], F32))
        PS = E(nc.psum_tensor("PS", [128, 8, 512], F32))
        TMPa = lambda tb: SC[:, tb * 512:(tb + 1) * 512]
        STGa = lambda tb, q: SC[:, 1024 + tb * 512 + q * 128:1024 + tb * 512 + (q + 1) * 128]
        ROPEc = SC[:, 2048:2560]
        ROPEs = SC[:, 2560:3072]
        RSCR = SC[:, 3072:3584]
        AXB = SC[0:32, 3584:5632]

        for s in ("qw", "qx", "qo", "qa", "qg", "pe", "dv", "ac", "cc"):
            P.sems[s] = E(nc.semaphore(s))

        psb = lambda b, n=512: PS[:, b, 0:n]

        P.dma("qx", xres, x0)
        P.dma("qx", EYE[:], eye_in)
        P.dma("qx", BIAS[:, 0:32], l0kb)
        P.dma("qx", BIAS[:, 32:38], b0qkv)
        P.dma("qx", BIAS[:, 38:70], b0d)
        P.dma("qx", BIAS[:, 70:82], b1qkv)
        t_b = P.dma("qx", BIAS[:, 82:114], b1o)
        P.dma("qx", MSK[:, 0:NPAT * 128].rearrange("p (n k) -> p n k", k=128),
              m0.rearrange("n p k -> p n k"))
        P.dma("qx", MSK[:, NPAT * 128:(NPAT + 1) * 128], m1)
        P.wait("vector", "qx", t_b)
        P.op("vector", lambda e: e.tensor_scalar(
            out=BIAS[:, 0:32], in0=BIAS[:, 0:32], scalar1=0.5, scalar2=None,
            op0=ALU.mult), sem="dv")

        GTS = SC[0:16, 0:8]
        t_g = P.dma("qx", GTS, gT)
        for which, lsrc, dst in (("A", lAT.rearrange("n i r -> n (i r)"), aeffT),
                                 ("B", lBT.rearrange("n r o -> n (r o)"), beffT)):
            for ch in range(256):
                bank = ch % 2
                lb = ch % 2
                LCH = SC[0:16, 512 + lb * 512:1024 + lb * 512]
                P.claim(("lch", lb), "scalar")
                tl = P.dma("qa", LCH, lsrc[:, ch * 512:(ch + 1) * 512])
                P.claim(("ps", bank), "tensor")
                P.wait("tensor", "qx", t_g)
                P.wait("tensor", "qa", tl)
                c_mm = P.op("tensor", lambda e, b=bank, LCH=LCH: e.matmul(
                    PS[0:8, b, :], GTS, LCH, start=True, stop=True), sem="pe")
                P.consume(("lch", lb), "pe", c_mm)
                P1S = SC[0:8, 2048 + bank * 512:2560 + bank * 512]
                P.claim(("p1stg", bank), "vector")
                P.wait("vector", "pe", c_mm)
                c_ev = P.op("vector", lambda e, b=bank, P1S=P1S: e.tensor_copy(
                    P1S, PS[0:8, b, :]), sem="dv")
                P.consume(("ps", bank), "dv", c_ev)
                P.wait("sync", "dv", c_ev)
                if which == "A":
                    i0 = ch * 16
                    kcq, pq = divmod(i0, 128)
                    t_w = P.dma("qo", dst[:, kcq, pq:pq + 16, :], P1S)
                else:
                    r0, o0 = divmod(ch * 512, D)
                    t_w = P.dma("qo", dst[:, r0, o0:o0 + 512], P1S)
                P.consume(("p1stg", bank), "qo", t_w)

        P.consume(("sc",), "pe", P.cnt["pe"])
        P.consume(("sc",), "qo", P.cnt["qo"])

        def unpack_z(qz, G, NPK, NN, zdst, nq):
            ZB = XNT[0:G, 0:NPK]
            t0 = P.dma("qx", ZB.bitcast(I32), qz)
            tlast = 0
            W = NN // nq
            for q in range(nq):
                ZU = XNT[0:G, 4096:4096 + W]
                ZF = XNT[0:G, 12288:12288 + W]
                P.wait("vector", "qx", t0)
                last = None
                for j in range(8):
                    last = P.op("vector", lambda e, j=j, q=q, W=W, ZU=ZU: e.tensor_scalar(
                        out=ZU.bitcast(I32).rearrange("p (a e) -> p a e", e=8)[:, :, j],
                        in0=ZB.bitcast(I32)[:, q * (W // 8):(q + 1) * (W // 8)],
                        scalar1=4 * j, scalar2=15,
                        op0=ALU.logical_shift_right, op1=ALU.bitwise_and), sem="dv")
                P.wait("scalar", "dv", last)
                c = P.op("scalar", lambda e, ZF=ZF, ZU=ZU: e.copy(ZF, ZU.bitcast(I32)), sem="ac")
                P.wait("sync", "ac", c)
                tlast = P.dma("qo", zdst[:, q * W:(q + 1) * W], ZF)
                P.wait("vector", "ac", c)
                P.wait("vector", "qo", tlast)
            P.consume(("xnt",), "qo", tlast)
            return tlast

        def dequant():
            tz_g = unpack_z(qzg, 32, 2752, 22016, zdq_g, 4)
            tz_d = unpack_z(qzd, IC, 512, D, zdq_d, 1)
            def dq(qw, GN, NN, nstr, sc_in, zdq, wdst, tz, tile_per_strip):
                W = NN // nstr
                PK = W // 8
                for g in range(GN):
                    for s in range(nstr):
                        ZR = XNT[:, 0:W]
                        SR = XNT[:, 5504:5504 + W]
                        UN = XNT[:, 11008:11008 + W]
                        FW = XNT[:, 16512:16512 + W]
                        qb = SC[:, 0:PK]
                        P.claim(("dqqb",), "sync")
                        t1 = P.dma("qw", qb.bitcast(I32), qw[g * 128:(g + 1) * 128, s * PK:(s + 1) * PK])
                        P.wait("scalar", "qo", tz)
                        P.claim(("dqzr",), "scalar")
                        t2 = P.dma("qa", ZR, zdq[g:g + 1, s * W:(s + 1) * W].broadcast_to([128, W]))
                        t3 = P.dma("qa", SR, sc_in[g:g + 1, s * W:(s + 1) * W].broadcast_to([128, W]))
                        P.wait("vector", "qw", t1)
                        P.claim(("dqun",), "vector")
                        last = None
                        for j in range(8):
                            last = P.op("vector", lambda e, j=j, UN=UN, qb=qb: e.tensor_scalar(
                                out=UN.bitcast(I32).rearrange("p (a e) -> p a e", e=8)[:, :, j],
                                in0=qb.bitcast(I32), scalar1=4 * j, scalar2=15,
                                op0=ALU.logical_shift_right, op1=ALU.bitwise_and), sem="dv")
                        P.consume(("dqqb",), "dv", last)
                        P.wait("scalar", "dv", last)
                        P.claim(("dqfw",), "scalar")
                        c1 = P.op("scalar", lambda e, FW=FW, UN=UN: e.copy(FW, UN.bitcast(I32)), sem="ac")
                        P.consume(("dqun",), "ac", c1)
                        P.wait("vector", "ac", c1)
                        P.wait("vector", "qa", t3)
                        c2 = P.op("vector", lambda e, FW=FW, ZR=ZR: e.tensor_tensor(
                            out=FW, in0=FW, in1=ZR, op=ALU.subtract), sem="dv")
                        c3 = P.op("vector", lambda e, FW=FW, SR=SR: e.tensor_tensor(
                            out=FW, in0=FW, in1=SR, op=ALU.mult), sem="dv")
                        P.consume(("dqzr",), "dv", c3)
                        P.wait("sync", "dv", c3)
                        nch0 = (s * W) // 128
                        ncnt = W // 128
                        t4 = 0
                        step = (ncnt + 3) // 4
                        for piece in range(0, ncnt, step):
                            pc = min(step, ncnt - piece)
                            wv, nof = wdst(nch0 + piece)
                            t4 = P.dma("qo",
                                       wv[nof:nof + pc, :, g * 128:(g + 1) * 128]
                                       .rearrange("n p e -> p n e"),
                                       FW[:, piece * 128:(piece + pc) * 128]
                                       .rearrange("p (n e) -> p n e", e=128))
                        P.consume(("dqfw",), "qo", t4)
                P.consume(("xnt",), "qo", P.cnt["qo"])
                P.consume(("xnt",), "dv", P.cnt["dv"])
                P.consume(("xnt",), "ac", P.cnt["ac"])
            dq(qwg, 32, 22016, 4, scg, zdq_g, wgu, tz_g, 43)
            dq(qwd, IC, D, 1, scd, zdq_d, wdn, tz_d, 32)

        res_wm = {"qg": 0}

        def ln_phase(tiles, span):
            P.claim_multi(("xnt",), ("vector", "scalar"))
            for jj, j in enumerate(tiles):
                XT = SC[:, 0:4096]
                XN = SC[:, 4096:8192]
                P.claim(("sc",), "sync")
                P.wait("sync", "qg", res_wm["qg"])
                t1 = P.dma("qx", XT, xres[j])
                P.claim(("stat",), "vector")
                P.wait("vector", "qx", t1)
                c1 = P.op("vector", lambda e, XT=XT: e.reduce_sum(STAT[:, 0:1], XT, axis=AXX), sem="dv")
                c2 = P.op("vector", lambda e: e.tensor_scalar(
                    out=STAT[:, 1:2], in0=STAT[:, 0:1], scalar1=-1.0 / D, scalar2=None,
                    op0=ALU.mult), sem="dv")
                P.wait("scalar", "dv", c2)
                c3 = P.op("scalar", lambda e, XT=XT, XN=XN: e.activation(
                    XN, XT, AF.Square, bias=STAT[:, 1:2], scale=1.0,
                    accum_out=STAT[:, 2:3]), sem="ac")
                P.wait("vector", "ac", c3)
                c4 = P.op("vector", lambda e: e.tensor_scalar(
                    out=STAT[:, 3:4], in0=STAT[:, 2:3], scalar1=1.0 / D, scalar2=EPS,
                    op0=ALU.mult, op1=ALU.add), sem="dv")
                P.wait("scalar", "dv", c4)
                c5 = P.op("scalar", lambda e: e.activation(
                    STAT[:, 4:5], STAT[:, 3:4], AF.Sqrt), sem="ac")
                P.wait("vector", "ac", c5)
                c6 = P.op("vector", lambda e: e.reciprocal(STAT[:, 5:6], STAT[:, 4:5]), sem="dv")
                c7 = P.op("vector", lambda e: e.tensor_tensor(
                    out=STAT[:, 6:7], in0=STAT[:, 1:2], in1=STAT[:, 5:6], op=ALU.mult), sem="dv")
                P.wait("scalar", "dv", c7)
                c8 = P.op("scalar", lambda e, XT=XT, XN=XN: e.activation(
                    XN, XT, AF.Identity, bias=STAT[:, 6:7], scale=STAT[:, 5:6]), sem="ac")
                P.consume(("stat",), "ac", c8)
                lastpe = 0
                for c in range(KC):
                    bank = 4 + (c % 4)
                    P.claim(("ps", bank), "tensor")
                    P.wait("tensor", "ac", c8)
                    cm = P.op("tensor", lambda e, c=c, bank=bank, XN=XN: e.transpose(
                        PS[:, bank, 0:128], XN[:, c * 128:(c + 1) * 128], EYE[:]), sem="pe")
                    lastpe = cm
                    eng, sm = (("vector", "dv") if c % 2 == 0 else ("scalar", "ac"))
                    P.wait(eng, "pe", cm)
                    dst = XNT[:, c * span + jj * 128: c * span + (jj + 1) * 128]
                    if eng == "vector":
                        ce = P.op(eng, lambda e, dst=dst, bank=bank: e.tensor_copy(dst, PS[:, bank, 0:128]), sem=sm)
                    else:
                        ce = P.op(eng, lambda e, dst=dst, bank=bank: e.copy(dst, PS[:, bank, 0:128]), sem=sm)
                    P.consume(("ps", bank), sm, ce)
                P.consume(("sc",), "ac", c8)
                P.consume(("sc",), "pe", lastpe)

        def resid_write(bank, j0, nch, scale, bias_col, half):
            tb = half
            bias_ap = BIAS[:, bias_col:bias_col + 1] if bias_col is not None else 0.0
            P.claim(("tmp", tb), "scalar")
            ce = P.op("scalar", lambda e, tb=tb, bank=bank, bias_ap=bias_ap, scale=scale: e.activation(
                TMPa(tb), psb(bank), AF.Identity, bias=bias_ap, scale=scale), sem="ac")
            P.consume(("ps", bank), "ac", ce)
            lastpe = None
            for q in range(4):
                pb = 4 + (q % 2) + 2 * tb
                P.claim(("ps", pb), "tensor")
                P.wait("tensor", "ac", ce)
                cm = P.op("tensor", lambda e, q=q, pb=pb, tb=tb: e.transpose(
                    PS[:, pb, 0:128], TMPa(tb)[:, q * 128:(q + 1) * 128], EYE[:]), sem="pe")
                P.consume(("tmp", tb), "pe", cm)
                lastpe = cm
                P.claim(("stg2", tb, q), "vector")
                P.wait("vector", "pe", cm)
                cv = P.op("vector", lambda e, q=q, pb=pb, tb=tb: e.tensor_copy(
                    STGa(tb, q), PS[:, pb, 0:128]), sem="dv")
                P.consume(("ps", pb), "dv", cv)
                P.wait("gpsimd", "dv", cv)
                tq = P.dma("qg", xres[j0 + q, :, nch * 128:(nch + 1) * 128],
                           STGa(tb, q), accum_op=ALU.add)
                P.consume(("stg2", tb, q), "qg", tq)
            P.consume(("sc",), "pe", lastpe)
            P.consume(("sc",), "qg", P.cnt["qg"])
            res_wm["qg"] = P.cnt["qg"]

        def l0ffn():
            ln_phase(list(range(NT)), TPC)
            for j in range(NT):
                bank = j % 2
                P.claim(("ps", bank), "tensor")
                cm = None
                for kc in range(KC):
                    slot = (j * KC + kc) % 8
                    P.claim(("aef", slot), "scalar")
                    tl = P.dma("qa", AEF[:, slot, :], aeffT[j, kc])
                    P.wait("tensor", "qa", tl)
                    cm = P.op("tensor", lambda e, kc=kc, slot=slot, bank=bank, j=j: e.matmul(
                        PS[0:32, bank, 0:128], AEF[:, slot, :],
                        XNT[:, kc * TPC + j * 128: kc * TPC + (j + 1) * 128],
                        start=(kc == 0), stop=(kc == KC - 1)), sem="pe")
                    P.consume(("aef", slot), "pe", cm)
                P.wait("vector", "pe", cm)
                cv = P.op("vector", lambda e, j=j, bank=bank: e.tensor_copy(
                    AXB[:, j * 128:(j + 1) * 128], PS[0:32, bank, 0:128]), sem="dv")
                P.consume(("ps", bank), "dv", cv)
            for h in range(2):
                bank = 2 + h
                P.claim(("ps", bank), "tensor")
                for kc in range(KC):
                    slot = (NT * KC + h * KC + kc) % 8
                    P.claim(("aef", slot), "scalar")
                    tl = P.dma("qa", AEF[:, slot, :], lAsT[kc])
                    P.wait("tensor", "qa", tl)
                    cm = P.op("tensor", lambda e, kc=kc, h=h, bank=bank, slot=slot: e.matmul(
                        PS[0:32, bank, :], AEF[:, slot, :],
                        XNT[:, kc * TPC + h * 512: kc * TPC + h * 512 + 512],
                        start=(kc == 0), stop=(kc == KC - 1)), sem="pe")
                    P.consume(("aef", slot), "pe", cm)
                P.wait("vector", "pe", cm)
                cv = P.op("vector", lambda e, h=h, bank=bank: e.tensor_copy(
                    AXB[:, 1024 + h * 512:1024 + h * 512 + 512], PS[0:32, bank, :]), sem="dv")
                P.consume(("ps", bank), "dv", cv)
            cax = P.cnt["dv"]
            for nch in range(KC):
                wbuf = nch % 2
                P.claim(("wb", wbuf), "sync")
                tw = P.dma("qw", WB[:, wbuf, :], l0k[nch])
                P.claim(("bef", wbuf), "scalar")
                P.dma("qa", BEF[:, wbuf, 0:1024].rearrange("r (g o) -> r g o", g=8),
                      beffT[:, :, nch * 128:(nch + 1) * 128].rearrange("g r o -> r g o"))
                tbf = P.dma("qa", BEF[:, wbuf, 1024:1152], lBsT[:, nch * 128:(nch + 1) * 128])
                for h in range(2):
                    bank = 2 * (nch % 2) + h
                    P.claim(("ps", bank), "tensor")
                    P.wait("tensor", "qw", tw)
                    for kc in range(KC):
                        P.op("tensor", lambda e, kc=kc, h=h, bank=bank, wbuf=wbuf: e.matmul(
                            psb(bank), WB[:, wbuf, kc * 128:(kc + 1) * 128],
                            XNT[:, kc * TPC + h * 512: kc * TPC + h * 512 + 512],
                            start=(kc == 0), stop=False))
                    P.wait("tensor", "dv", cax)
                    P.wait("tensor", "qa", tbf)
                    for q in range(4):
                        j = h * 4 + q
                        P.op("tensor", lambda e, j=j, q=q, bank=bank, wbuf=wbuf: e.matmul(
                            PS[:, bank, q * 128:(q + 1) * 128], BEF[:, wbuf, j * 128:(j + 1) * 128],
                            AXB[:, j * 128:(j + 1) * 128], start=False, stop=False))
                    cm = P.op("tensor", lambda e, h=h, bank=bank, wbuf=wbuf: e.matmul(
                        psb(bank), BEF[:, wbuf, 1024:1152],
                        AXB[:, 1024 + h * 512:1024 + h * 512 + 512],
                        start=False, stop=True), sem="pe")
                    P.consume(("wb", wbuf), "pe", cm)
                    P.consume(("bef", wbuf), "pe", cm)
                    P.consume(("sc",), "pe", cm)
                    P.wait("scalar", "pe", cm)
                    resid_write(bank, h * 4, nch, 0.5, nch, h)


        def attn(layer):
            if layer == 0:
                nch_tot, rq, rk = 6, [0, 1, 2, 3], [4]
                vch = {5: 0}
                wq, wo = w0qkv, w0d
                bq_col, bo_col = 32, 38
                rcos, rsin = r0cos, r0sin
                scale = L0_SCALE
                heads = [(b, 0) for b in range(2)]
                qof = lambda hl, qh: qh
                kof = lambda hl: 4
                vof = lambda hl: 5 if False else 0
                nqh = 4
                att_of = lambda i: L0_ATT[i]
            else:
                nch_tot, rq, rk = 12, [0, 1, 2, 3], [4, 5, 6, 7]
                vch = {8: 0, 9: 1, 10: 2, 11: 3}
                wq, wo = w1qkv, w1o
                bq_col, bo_col = 70, 82
                rcos, rsin = r1cos, r1sin
                scale = L1_SCALE
                heads = [(b, hl) for b in range(2) for hl in range(4)]
                nqh = 1
                att_of = lambda i: [(kt, (-1 if kt < i else NPAT)) for kt in range(i + 1)]

            ln_phase(list(range(NT)), TPC)
            dvE, acE = P.cnt["dv"], P.cnt["ac"]
            P.wait("sync", "dv", dvE)
            P.wait("sync", "ac", acE)
            t_ci = 0
            P.claim(("ccin",), "sync")
            for kq in range(0, KC, 8):
                t_ci = P.dma("qo", cc_in[kq:kq + 8].rearrange("kc p t -> p kc t"),
                             XNT[:, kq * TPC:(kq + 8) * TPC].rearrange("p (kc t) -> p kc t", t=TPC))
            P.consume(("xnt",), "qo", t_ci)
            P.wait("gpsimd", "qo", t_ci)
            P.claim(("agout",), "gpsimd")
            c_ag = P.op("gpsimd", lambda e: e.collective_compute(
                "AllGather", ALU.bypass, replica_groups=[list(range(NCORE))],
                ins=[cc_in.opt()], outs=[ag_out.opt()]), sem="cc")
            P.consume(("ccin",), "cc", c_ag)

            QA = lambda par: SC[:, par * 512:(par + 1) * 512]
            QR = lambda par: SC[:, 1024 + par * 512:1536 + par * 512]
            RC = SC[:, 2048:2560]
            RS = SC[:, 2560:3072]
            VST = lambda par: SC[:, 3072 + par * 512:3584 + par * 512]
            WS = lambda par: SC[:, 4096 + par * 512:4608 + par * 512]
            P.claim_multi(("sc",), ("scalar", "vector", "sync"))
            uc = 0
            for blk in range(NCORE):
                P.claim(("xnt",), "sync")
                P.wait("sync", "cc", c_ag)
                tg = 0
                for kq in range(0, KC, 8):
                    tg = P.dma("qx", XNT[:, kq * TPC:(kq + 8) * TPC].rearrange("p (kc t) -> p kc t", t=TPC),
                               ag_out[blk, kq:kq + 8].rearrange("kc p t -> p kc t"))
                for nch in range(nch_tot):
                    wb = nch % 2
                    P.claim(("wb", wb), "sync")
                    tw = P.dma("qw", WB[:, wb, :], wq[nch])
                    for tb in range(2):
                        if nch == 0 and (nch_tot > 1):
                            pass
                        bank = 2 * (nch % 2) + tb
                        P.claim(("ps", bank), "tensor")
                        P.wait("tensor", "qw", tw)
                        P.wait("tensor", "qx", tg)
                        cm = None
                        for kc in range(KC):
                            cm = P.op("tensor", lambda e, kc=kc, tb=tb, bank=bank, wb=wb: e.matmul(
                                psb(bank), WB[:, wb, kc * 128:(kc + 1) * 128],
                                XNT[:, kc * TPC + tb * 512: kc * TPC + tb * 512 + 512],
                                start=(kc == 0), stop=(kc == KC - 1)),
                                sem=("pe" if kc == KC - 1 else None))
                        P.consume(("wb", wb), "pe", cm)
                        P.consume(("xnt",), "pe", cm)
                        tok0 = blk * TPC + tb * 512
                        par = uc % 2
                        uc += 1
                        if nch in rq or nch in rk:
                            P.claim(("rc",), "scalar")
                            trc = P.dma("qa", RC, rcos[:, tok0:tok0 + 512])
                            trs = P.dma("qa", RS, rsin[:, tok0:tok0 + 512])
                            P.claim(("qa", par), "scalar")
                            P.wait("scalar", "pe", cm)
                            ce = P.op("scalar", lambda e, par=par, bank=bank, nch=nch: e.activation(
                                QA(par), psb(bank), AF.Identity,
                                bias=BIAS[:, bq_col + nch:bq_col + nch + 1], scale=1.0), sem="ac")
                            P.consume(("ps", bank), "ac", ce)
                            P.claim(("qr", par), "scalar")
                            P.wait("scalar", "ac", ce)
                            tr1 = P.dma("qa", QR(par)[0:64, :], QA(par)[64:128, :])
                            tr2 = P.dma("qa", QR(par)[64:128, :], QA(par)[0:64, :])
                            P.wait("vector", "qa", tr2)
                            P.wait("vector", "ac", ce)
                            c1 = P.op("vector", lambda e, par=par: e.tensor_tensor(
                                out=QA(par), in0=QA(par), in1=RC, op=ALU.mult), sem="dv")
                            c2 = P.op("vector", lambda e, par=par: e.tensor_tensor(
                                out=QR(par), in0=QR(par), in1=RS, op=ALU.mult), sem="dv")
                            P.consume(("rc",), "dv", c2)
                            c3 = P.op("vector", lambda e, par=par: e.tensor_tensor(
                                out=QA(par), in0=QA(par), in1=QR(par), op=ALU.add), sem="dv")
                            P.consume(("qr", par), "dv", c3)
                            P.wait("sync", "dv", c3)
                            t_o = P.dma("qo", qkvT[nch, :, tok0:tok0 + 512], QA(par))
                            P.consume(("qa", par), "qo", t_o)
                            P.consume(("qa", par), "dv", c3)
                        elif nch in vch:
                            vl = vch[nch]
                            P.claim(("vst", par), "scalar")
                            P.wait("scalar", "pe", cm)
                            ce = P.op("scalar", lambda e, par=par, bank=bank, nch=nch: e.activation(
                                VST(par), psb(bank), AF.Identity,
                                bias=BIAS[:, bq_col + nch:bq_col + nch + 1], scale=1.0), sem="ac")
                            P.consume(("ps", bank), "ac", ce)
                            for q in range(4):
                                pb = 4 + (q % 2) + 2 * (tb % 2)
                                P.claim(("ps", pb), "tensor")
                                P.wait("tensor", "ac", ce)
                                cmq = P.op("tensor", lambda e, q=q, pb=pb, par=par: e.transpose(
                                    PS[:, pb, 0:128], VST(par)[:, q * 128:(q + 1) * 128], EYE[:]), sem="pe")
                                P.consume(("vst", par), "pe", cmq)
                                wsp = q % 2
                                P.claim(("ws", wsp), "vector")
                                P.wait("vector", "pe", cmq)
                                cv = P.op("vector", lambda e, q=q, pb=pb, wsp=wsp: e.tensor_copy(
                                    WS(wsp)[:, 0:128] if False else SC[:, 4096 + wsp * 512:4096 + wsp * 512 + 128],
                                    PS[:, pb, 0:128]), sem="dv")
                                P.consume(("ps", pb), "dv", cv)
                                gt = blk * NT + tb * 4 + q
                                P.wait("sync", "dv", cv)
                                t_o = P.dma("qo", vT_tm[vl, gt], SC[:, 4096 + wsp * 512:4096 + wsp * 512 + 128])
                                P.consume(("ws", wsp), "qo", t_o)
                        else:
                            raise AssertionError("unmapped chunk")
            P.consume(("sc",), "dv", P.cnt["dv"])
            P.consume(("sc",), "ac", P.cnt["ac"])
            P.consume(("sc",), "qo", P.cnt["qo"])
            P.consume(("sc",), "qa", P.cnt["qa"])

            QTA = lambda pr: XNT[:, pr * 20480:pr * 20480 + 4096]
            KTA = lambda pr: XNT[:, 4096 + pr * 20480:8192 + pr * 20480]
            VBA = lambda pr: XNT[:, 8192 + pr * 20480:12288 + pr * 20480]
            SCOR2 = lambda par: XNT[:, 12288 + par * 4096:16384 + par * 4096]

            def emit_kv_loads(pi, q):
                b, hl = heads[pi]
                pr = pi % 2 if layer == 1 else 0
                kch = 4 + (0 if layer == 0 else hl)
                vl = 0 if layer == 0 else hl
                P.wait(QSEMS[q], "qo", 1)
                P.claim(("xk", pr), QSEMS[q])
                P.claim(("xv", pr), QSEMS[q])
                t = 0
                for i in range(TB):
                    t = P.dma(q, KTA(pr)[:, i * 128:(i + 1) * 128],
                              qkvT[kch, :, gcol(b, i):gcol(b, i) + 128])
                    t = P.dma(q, VBA(pr)[:, i * 128:(i + 1) * 128], vT_tm[vl, gtile(b, i)])
                if layer == 1:
                    emit_q_loads(pi, 0, q)
                return t

            def emit_q_loads(pi, qh, q):
                b, hl = heads[pi]
                pr = pi % 2 if layer == 1 else qh % 2
                qch = qh if layer == 0 else hl
                P.wait(QSEMS[q], "qo", 1)
                P.claim(("xq", pr), QSEMS[q])
                t = 0
                for i in range(TB):
                    t = P.dma(q, QTA(pr)[:, i * 128:(i + 1) * 128],
                              qkvT[qch, :, gcol(b, i):gcol(b, i) + 128])
                return t

            def emit_scores(pi, qh, qt):
                b, hl = heads[pi]
                pr = pi % 2 if layer == 1 else qh % 2
                kpr = pi % 2 if layer == 1 else 0
                par = qt % 2
                lst = att_of(qt)
                runs, run = [], [lst[0]]
                for kt, pat in lst[1:]:
                    if kt == run[-1][0] + 1 and len(run) < 4:
                        run.append((kt, pat))
                    else:
                        runs.append(run); run = [(kt, pat)]
                runs.append(run)
                P.claim_multi(("scor", par), ("scalar", "vector"))
                ci = 0
                for rb, run in enumerate(runs):
                    bank = rb % 2
                    k0 = run[0][0]
                    w = len(run) * 128
                    P.claim(("ps", bank), "tensor")
                    P.wait("tensor", "qx", 1)
                    P.wait("tensor", "qg", 1)
                    cm = P.op("tensor", lambda e, qt=qt, k0=k0, w=w, bank=bank, pr=pr, kpr=kpr: e.matmul(
                        PS[:, bank, 0:w], QTA(pr)[:, qt * 128:(qt + 1) * 128],
                        KTA(kpr)[:, k0 * 128:k0 * 128 + w], start=True, stop=True), sem="pe")
                    P.consume(("xq", pr), "pe", cm)
                    P.consume(("xk", kpr), "pe", cm)
                    if all(p < 0 for _, p in run):
                        P.wait("scalar", "pe", cm)
                        ce = P.op("scalar", lambda e, ci=ci, w=w, bank=bank, par=par: e.copy(
                            SCOR2(par)[:, ci * 128:ci * 128 + w], PS[:, bank, 0:w]), sem="ac")
                        P.consume(("ps", bank), "ac", ce)
                    else:
                        P.wait("vector", "pe", cm)
                        ce = None
                        for x_, (kt, pat) in enumerate(run):
                            if pat < 0:
                                ce = P.op("vector", lambda e, ci=ci, x_=x_, bank=bank, par=par: e.tensor_copy(
                                    SCOR2(par)[:, (ci + x_) * 128:(ci + x_ + 1) * 128],
                                    PS[:, bank, x_ * 128:(x_ + 1) * 128]), sem="dv")
                            else:
                                ce = P.op("vector", lambda e, ci=ci, x_=x_, bank=bank, pat=pat, par=par: e.tensor_tensor(
                                    out=SCOR2(par)[:, (ci + x_) * 128:(ci + x_ + 1) * 128],
                                    in0=PS[:, bank, x_ * 128:(x_ + 1) * 128],
                                    in1=MSK[:, pat * 128:(pat + 1) * 128], op=ALU.add), sem="dv")
                        P.consume(("ps", bank), "dv", ce)
                    ci += len(run)
                return lst

            def emit_rest(pi, qh, qt, lst):
                b, hl = heads[pi]
                pr = pi % 2 if layer == 1 else 0
                par = qt % 2
                so = 8 + 8 * par
                natt = len(lst)
                c1 = P.op("vector", lambda e, natt=natt, par=par, so=so: e.reduce_max(
                    STAT[:, so:so + 1], SCOR2(par)[:, 0:natt * 128], axis=AXX), sem="dv")
                c2 = P.op("vector", lambda e, so=so: e.tensor_scalar(
                    out=STAT[:, so + 1:so + 2], in0=STAT[:, so:so + 1], scalar1=-scale,
                    scalar2=None, op0=ALU.mult), sem="dv")
                P.wait("scalar", "dv", c2)
                c3 = P.op("scalar", lambda e, natt=natt, par=par, so=so: e.activation(
                    SCOR2(par)[:, 0:natt * 128], SCOR2(par)[:, 0:natt * 128], AF.Exp,
                    bias=STAT[:, so + 1:so + 2], scale=scale,
                    accum_out=STAT[:, so + 2:so + 3]), sem="ac")
                P.wait("vector", "ac", c3)
                c4 = P.op("vector", lambda e, so=so: e.reciprocal(
                    STAT[:, so + 3:so + 4], STAT[:, so + 2:so + 3]), sem="dv")
                P.wait("scalar", "dv", c4)
                c5 = P.op("scalar", lambda e, natt=natt, par=par, so=so: e.activation(
                    SCOR2(par)[:, 0:natt * 128], SCOR2(par)[:, 0:natt * 128], AF.Identity,
                    scale=STAT[:, so + 3:so + 4]), sem="ac")
                cb = 2 + par
                P.claim(("ps", cb), "tensor")
                lastmm = None
                lastt = None
                for c_, (kt, pat) in enumerate(lst):
                    pb = 4 + (c_ % 4)
                    P.claim(("ps", pb), "tensor")
                    P.wait("tensor", "ac", c5)
                    cmt = P.op("tensor", lambda e, c_=c_, pb=pb, par=par: e.transpose(
                        PS[:, pb, 0:128], SCOR2(par)[:, c_ * 128:(c_ + 1) * 128], EYE[:]), sem="pe")
                    lastt = cmt
                    sl = c_ % 4
                    P.claim(("ptb", sl), "vector")
                    P.wait("vector", "pe", cmt)
                    cv = P.op("vector", lambda e, sl=sl, pb=pb: e.tensor_copy(
                        PTB[:, sl, :], PS[:, pb, 0:128]), sem="dv")
                    P.consume(("ps", pb), "dv", cv)
                    P.wait("tensor", "dv", cv)
                    lastmm = P.op("tensor", lambda e, kt=kt, sl=sl, cb=cb, c_=c_, natt=natt, pr=pr: e.matmul(
                        PS[:, cb, 0:128], VBA(pr)[:, kt * 128:(kt + 1) * 128],
                        PTB[:, sl, :], start=(c_ == 0), stop=(c_ == natt - 1)), sem="pe")
                    P.consume(("ptb", sl), "pe", lastmm)
                P.consume(("xv", pr), "pe", lastmm)
                P.consume(("scor", par), "pe", lastt)
                P.consume(("scor", par), "ac", c5)
                wsp = par
                P.claim(("ws2", wsp), "vector")
                P.wait("vector", "pe", lastmm)
                cv = P.op("vector", lambda e, wsp=wsp, cb=cb: e.tensor_copy(
                    SC[:, 5120 + wsp * 512:5120 + wsp * 512 + 128], PS[:, cb, 0:128]), sem="dv")
                P.consume(("ps", cb), "dv", cv)
                chl = (hl * nqh + qh) if layer == 0 else hl
                P.wait("sync", "dv", cv)
                t_o = P.dma("qo", ctxT_sc[chl, :, gcol(b, qt):gcol(b, qt) + 128],
                            SC[:, 5120 + wsp * 512:5120 + wsp * 512 + 128])
                P.consume(("ws2", wsp), "qo", t_o)
                P.consume(("sc",), "qo", t_o)

            P.claim(("xnt",), "sync")
            prev = None
            for pi in range(len(heads)):
                if pi == 0 or layer == 0:
                    if prev is not None:
                        emit_rest(*prev)
                        prev = None
                    emit_kv_loads(pi, "qx")
                    if layer == 0:
                        emit_q_loads(pi, 0, "qx")
                for qh in range(nqh):
                    for qt in range(TB):
                        lst = emit_scores(pi, qh, qt)
                        if prev is not None:
                            emit_rest(*prev)
                        prev = (pi, qh, qt, lst)
                        if qh == 0 and qt == 0 and layer == 1 and pi + 1 < len(heads):
                            emit_kv_loads(pi + 1, "qg")
                        if layer == 0 and qt == 0 and qh + 1 < nqh:
                            emit_q_loads(pi, qh + 1, "qg")
            emit_rest(*prev)
            for pr in (0, 1):
                P.consume(("xq", pr), "pe", P.cnt["pe"])
                P.consume(("xk", pr), "pe", P.cnt["pe"])
                P.consume(("xv", pr), "pe", P.cnt["pe"])
            for par in (0, 1):
                P.consume(("scor", par), "pe", P.cnt["pe"])
            P.consume(("xnt",), "pe", P.cnt["pe"])

            P.claim(("xnt",), "sync")
            P.wait("sync", "qo", P.cnt["qo"])
            tg = 0
            for chl in range(4):
                tg = P.dma("qx", XNT[:, chl * 8192:(chl + 1) * 8192], ctxT_sc[chl])
            P.consume(("ctx",), "qx", tg)
            for nch in range(KC):
                wb = nch % 2
                P.claim(("wb", wb), "sync")
                tw = P.dma("qw", WB[:, wb, 0:512], wo[nch])
                for tbk in range(16):
                    bank = 2 * (nch % 2) + tbk % 2
                    P.claim(("ps", bank), "tensor")
                    P.wait("tensor", "qw", tw)
                    P.wait("tensor", "qx", tg)
                    cm = None
                    for kc in range(4):
                        cm = P.op("tensor", lambda e, kc=kc, tbk=tbk, bank=bank, wb=wb: e.matmul(
                            psb(bank), WB[:, wb, kc * 128:(kc + 1) * 128],
                            XNT[:, kc * 8192 + tbk * 512: kc * 8192 + tbk * 512 + 512],
                            start=(kc == 0), stop=(kc == 3)),
                            sem=("pe" if kc == 3 else None))
                    P.consume(("wb", wb), "pe", cm)
                    P.consume(("xnt",), "pe", cm)
                    par = tbk % 2
                    P.claim(("ops", par), "vector")
                    P.wait("vector", "pe", cm)
                    cv = P.op("vector", lambda e, par=par, bank=bank: e.tensor_copy(
                        SC[:, 5120 + par * 512:5120 + par * 512 + 512], psb(bank)), sem="dv")
                    P.consume(("ps", bank), "dv", cv)
                    P.wait("sync", "dv", cv)
                    t_o = P.dma("qo", rs_in[tbk // 2, nch, :, (tbk % 2) * 512:(tbk % 2) * 512 + 512],
                                SC[:, 5120 + par * 512:5120 + par * 512 + 512])
                    P.consume(("ops", par), "qo", t_o)
                    P.consume(("sc",), "qo", t_o)
            P.wait("gpsimd", "qo", P.cnt["qo"])
            P.claim(("rso",), "gpsimd")
            c_rs = P.op("gpsimd", lambda e: e.collective_compute(
                "ReduceScatter", ALU.add, replica_groups=[list(range(NCORE))],
                ins=[rs_in.opt()], outs=[rs_out.opt()]), sem="cc")
            for nch in range(KC):
                par = nch % 2
                P.claim(("rsb", par), "sync")
                P.wait("sync", "cc", c_rs)
                tr = P.dma("qx", SC[:, par * 1024:(par + 1) * 1024], rs_out[nch])
                P.claim(("rsc", par), "scalar")
                P.wait("scalar", "qx", tr)
                ce = P.op("scalar", lambda e, par=par, nch=nch: e.activation(
                    SC[:, 2048 + par * 1024:2048 + (par + 1) * 1024],
                    SC[:, par * 1024:(par + 1) * 1024], AF.Identity,
                    bias=BIAS[:, bo_col + nch:bo_col + nch + 1], scale=1.0), sem="ac")
                P.consume(("rsb", par), "ac", ce)
                for j in range(NT):
                    pb = 4 + (j % 4)
                    P.claim(("ps", pb), "tensor")
                    P.wait("tensor", "ac", ce)
                    cmt = P.op("tensor", lambda e, j=j, pb=pb, par=par: e.transpose(
                        PS[:, pb, 0:128],
                        SC[:, 2048 + par * 1024 + j * 128:2048 + par * 1024 + (j + 1) * 128],
                        EYE[:]), sem="pe")
                    P.consume(("rsc", par), "pe", cmt)
                    wsp = j % 2
                    P.claim(("rst", par, wsp), "vector")
                    P.wait("vector", "pe", cmt)
                    cv = P.op("vector", lambda e, pb=pb, par=par, wsp=wsp: e.tensor_copy(
                        SC[:, 4096 + (2 * par + wsp) * 128:4096 + (2 * par + wsp) * 128 + 128],
                        PS[:, pb, 0:128]), sem="dv")
                    P.consume(("ps", pb), "dv", cv)
                    P.wait("gpsimd", "dv", cv)
                    tq_ = P.dma("qg", xres[j, :, nch * 128:(nch + 1) * 128],
                                SC[:, 4096 + (2 * par + wsp) * 128:4096 + (2 * par + wsp) * 128 + 128],
                                accum_op=ALU.add)
                    P.consume(("rst", par, wsp), "qg", tq_)
                P.consume(("sc",), "pe", P.cnt["pe"])
                P.consume(("sc",), "qg", P.cnt["qg"])
            res_wm["qg"] = P.cnt["qg"]
            P.consume(("ctx",), "pe", P.cnt["pe"])
            P.consume(("rso",), "qx", P.cnt["qx"])


        def l1ffn():
            P.claim(("wb", 0), "sync")
            P.claim(("wb", 1), "sync")
            for h in range(2):
                tiles = [4 * h + q for q in range(4)]
                ln_phase(tiles, 512)
                OUT2 = lambda dch: XNT[:, 16384 + dch * 512:16384 + (dch + 1) * 512]
                XNH = lambda kc: XNT[:, kc * 512:(kc + 1) * 512]
                HG = lambda gi: SC[:, 4096 + gi * 512:4096 + (gi + 1) * 512]
                ngrp = (IC + 7) // 8
                wslot = [0]
                def wb_load(view, width):
                    s = wslot[0] % 4
                    wslot[0] += 1
                    dst = WB[:, s // 2, (s % 2) * 2048:(s % 2) * 2048 + width]
                    P.claim(("wb8", s), "sync")
                    t = P.dma("qw", dst, view)
                    return dst, s, t
                for gg in range(ngrp):
                    ilist = list(range(gg * 8, min(IC, gg * 8 + 8)))
                    for gi, i in enumerate(ilist):
                        gb = (2 * i) % 4
                        ub = (2 * i + 1) % 4
                        cmg = cmu = None
                        for part, bank in ((0, gb), (1, ub)):
                            nch = i if part == 0 else 86 + i
                            wv, nof = wgu(nch)
                            P.claim(("ps", bank), "tensor")
                            for kh in range(2):
                                dst, s, tw = wb_load(wv[nof, :, kh * 2048:(kh + 1) * 2048], 2048)
                                P.wait("tensor", "qw", tw)
                                cm = None
                                for kk in range(16):
                                    kc = kh * 16 + kk
                                    cm = P.op("tensor", lambda e, kk=kk, kc=kc, bank=bank, dst=dst: e.matmul(
                                        psb(bank), dst[:, kk * 128:(kk + 1) * 128], XNH(kc),
                                        start=(kc == 0), stop=(kc == KC - 1)),
                                        sem=("pe" if (kc == KC - 1 or kk == 15) else None))
                                P.consume(("wb8", s), "pe", cm)
                            if part == 0:
                                cmg = cm
                            else:
                                cmu = cm
                        par = i % 2
                        P.claim(("hgs", par), "scalar")
                        P.wait("scalar", "pe", cmg)
                        cs = P.op("scalar", lambda e, par=par, gb=gb: e.activation(
                            SC[:, par * 512:(par + 1) * 512], psb(gb), AF.Silu), sem="ac")
                        P.consume(("ps", gb), "ac", cs)
                        P.claim(("hg", gi), "vector")
                        P.wait("vector", "ac", cs)
                        P.wait("vector", "pe", cmu)
                        cv = P.op("vector", lambda e, par=par, ub=ub, gi=gi: e.tensor_tensor(
                            out=HG(gi), in0=SC[:, par * 512:(par + 1) * 512],
                            in1=psb(ub), op=ALU.mult), sem="dv")
                        P.consume(("ps", ub), "dv", cv)
                        P.consume(("hgs", par), "dv", cv)
                    cv_h = P.cnt["dv"]
                    for dch in range(KC):
                        bank = dch % 2
                        P.claim(("ps", bank), "tensor")
                        dst, s, tw = wb_load(
                            wdn_dq[dch, :, ilist[0] * 128:(ilist[-1] + 1) * 128], len(ilist) * 128)
                        P.wait("tensor", "qw", tw)
                        P.wait("tensor", "dv", cv_h)
                        cm = None
                        for gi in range(len(ilist)):
                            cm = P.op("tensor", lambda e, gi=gi, bank=bank, dst=dst: e.matmul(
                                psb(bank), dst[:, gi * 128:(gi + 1) * 128], HG(gi),
                                start=(gi == 0), stop=(gi == len(ilist) - 1)),
                                sem=("pe" if gi == len(ilist) - 1 else None))
                        P.consume(("wb8", s), "pe", cm)
                        for gi in range(len(ilist)):
                            P.consume(("hg", gi), "pe", cm)
                        P.claim(("o2", dch), "vector")
                        P.wait("vector", "pe", cm)
                        if gg == 0:
                            ce = P.op("vector", lambda e, dch=dch, bank=bank: e.tensor_copy(
                                OUT2(dch), psb(bank)), sem="dv")
                        else:
                            ce = P.op("vector", lambda e, dch=dch, bank=bank: e.tensor_tensor(
                                out=OUT2(dch), in0=OUT2(dch), in1=psb(bank), op=ALU.add), sem="dv")
                        P.consume(("ps", bank), "dv", ce)
                        P.consume(("o2", dch), "dv", ce) if False else None
                        P.consume(("xnt",), "dv", ce)
                for dch in range(KC):
                    for q in range(4):
                        pb = 4 + ((dch * 4 + q) % 4)
                        P.claim(("ps", pb), "tensor")
                        P.wait("tensor", "dv", P.cnt["dv"])
                        cmt = P.op("tensor", lambda e, dch=dch, q=q, pb=pb: e.transpose(
                            PS[:, pb, 0:128], OUT2(dch)[:, q * 128:(q + 1) * 128], EYE[:]), sem="pe")
                        P.consume(("xnt",), "pe", cmt)
                        wsp = (dch * 4 + q) % 4
                        P.claim(("l1s", wsp), "vector")
                        P.wait("vector", "pe", cmt)
                        cv = P.op("vector", lambda e, pb=pb, wsp=wsp: e.tensor_scalar(
                            out=SC[:, wsp * 128:(wsp + 1) * 128], in0=PS[:, pb, 0:128],
                            scalar1=0.5, scalar2=None, op0=ALU.mult), sem="dv")
                        P.consume(("ps", pb), "dv", cv)
                        P.wait("gpsimd", "dv", cv)
                        tq_ = P.dma("qg", xres[tiles[q], :, dch * 128:(dch + 1) * 128],
                                    SC[:, wsp * 128:(wsp + 1) * 128], accum_op=ALU.add)
                        P.consume(("l1s", wsp), "qg", tq_)
                        P.consume(("sc",), "qg", tq_)
                res_wm["qg"] = P.cnt["qg"]
            P.consume(("wb", 0), "pe", P.cnt["pe"])
            P.consume(("wb", 1), "pe", P.cnt["pe"])

        if N_SUB >= 4:
            dequant()
        if N_SUB >= 1:
            l0ffn()
        if N_SUB >= 2:
            attn(0)
        if N_SUB >= 3:
            l0ffn()
        if N_SUB >= 4:
            l1ffn()
        if N_SUB >= 5:
            attn(1)
        if N_SUB >= 6:
            l1ffn()

        P.wait("sync", "qg", res_wm["qg"])
        tout = P.dma("qo", out, xres)
        P.wait("sync", "qo", tout)

        with nc.Block() as block:
            block.sync(lambda e: [f(e) for f in P.st["sync"]])
            block.tensor(lambda e: [f(e) for f in P.st["tensor"]])
            block.vector(lambda e: [f(e) for f in P.st["vector"]])
            block.scalar(lambda e: [f(e) for f in P.st["scalar"]])
            block.gpsimd(lambda e: [f(e) for f in P.st["gpsimd"]])
    return nc


def kernel(**ins):
    x = np.ascontiguousarray(ins["L0_attn_hidden_states"], dtype=np.float32)
    xt = x.reshape(2, TB, 128, D)
    eye = np.eye(128, dtype=np.float32)
    r0c, r0s = _rope_tables(1000000.0)
    r1c, r1s = _rope_tables(10000.0)
    m1 = np.triu(np.full((128, 128), MASK1 / L1_SCALE, dtype=np.float32), 1)

    l0k_t = _tile_w(np.ascontiguousarray(ins["L0_ffn_kernel"], np.float32))
    w0qkvT = np.ascontiguousarray(ins["L0_attn_w_qkv"], np.float32).T
    w0dT = np.ascontiguousarray(ins["L0_attn_w_dense"], np.float32).T
    w1qkv = np.ascontiguousarray(ins["L1_attn_qkv_w"], np.float32)
    w1o_full = np.ascontiguousarray(ins["L1_attn_out_w"], np.float32)

    lAT = np.ascontiguousarray(ins["L0_ffn_lora_A"].transpose(0, 2, 1), np.float32)
    lBT = np.ascontiguousarray(ins["L0_ffn_lora_B"].transpose(0, 2, 1), np.float32)
    lAsT = np.ascontiguousarray(np.asarray(ins["L0_ffn_lora_A_shared"], np.float32).T
                                .reshape(KC, 128, 32))
    lBsT = np.ascontiguousarray(np.asarray(ins["L0_ffn_lora_B_shared"], np.float32).T)
    gT = np.ascontiguousarray(np.asarray(ins["L0_ffn_gate"], np.float32).T)

    Emb = D
    in_maps = []
    for c in range(NCORE):
        w0s = _tile_w(np.ascontiguousarray(w0qkvT[:, c * 768:(c + 1) * 768]))
        b0s = np.asarray(ins["L0_attn_b_qkv"][c * 768:(c + 1) * 768], np.float32)
        cols = [np.ascontiguousarray(
            w1qkv[:, sec * Emb + 4 * c * 128: sec * Emb + 4 * (c + 1) * 128])
            for sec in range(3)]
        w1s = _tile_w(np.concatenate(cols, axis=1))
        w0d_t = _tile_w(np.ascontiguousarray(w0dT[4 * c * 128:4 * (c + 1) * 128, :]))
        w1o_t = _tile_w(np.ascontiguousarray(w1o_full[4 * c * 128:4 * (c + 1) * 128, :]))
        b1s = np.concatenate([
            np.asarray(ins["L1_attn_qkv_b"][sec * Emb + 4 * c * 128: sec * Emb + 4 * (c + 1) * 128], np.float32)
            for sec in range(3)])
        in_maps.append({
            "x0": np.stack([xt[b, i] for (b, i) in GATHER[c]]),
            "eye": eye,
            "r0cos": r0c, "r0sin": r0s, "r1cos": r1c, "r1sin": r1s,
            "m0": L0_PATS, "m1": m1,
            "l0k": l0k_t,
            "l0kb": np.ascontiguousarray(np.asarray(ins["L0_ffn_bias"], np.float32).reshape(-1, 128).T),
            "gT": gT, "lAT": lAT, "lBT": lBT, "lAsT": lAsT, "lBsT": lBsT,
            "w0qkv": w0s,
            "b0qkv": np.ascontiguousarray(b0s.reshape(-1, 128).T),
            "w0d": w0d_t,
            "b0d": np.ascontiguousarray(np.asarray(ins["L0_attn_b_dense"], np.float32).reshape(-1, 128).T),
            "w1qkv": w1s,
            "b1qkv": np.ascontiguousarray(b1s.reshape(-1, 128).T),
            "w1o": w1o_t,
            "b1o": np.ascontiguousarray(np.asarray(ins["L1_attn_out_b"], np.float32).reshape(-1, 128).T),
            "qwg": np.ascontiguousarray(ins["L1_ffn_qweight_gu"]),
            "qzg": np.ascontiguousarray(ins["L1_ffn_qzeros_gu"]),
            "scg": np.ascontiguousarray(ins["L1_ffn_scales_gu"], np.float32),
            "qwd": np.ascontiguousarray(ins["L1_ffn_qweight_dn"]),
            "qzd": np.ascontiguousarray(ins["L1_ffn_qzeros_dn"]),
            "scd": np.ascontiguousarray(ins["L1_ffn_scales_dn"], np.float32),
        })

    nc = build_program()
    with allow_ici_all_to_all():
        res = run_bass_kernel_spmd(nc, in_maps, core_ids=list(range(NCORE)))

    y = np.empty((2, TB, 128, D), dtype=np.float32)
    for c in range(NCORE):
        o = res.results[c]["out"]
        for l, (b, i) in enumerate(GATHER[c]):
            y[b, i] = o[l]
    return y.reshape(2, SEQ, D)


# version 04 (final)
# speedup vs baseline: 1.0829x; 1.0829x over previous
import math
import numpy as np
from contextlib import ExitStack

import concourse.bass as bass
import concourse.mybir as mybir
from concourse.bass_utils import run_bass_kernel_spmd
from concourse.aot_env import allow_ici_all_to_all

F32 = mybir.dt.float32
I32 = mybir.dt.int32
BF16 = mybir.dt.bfloat16
AF = mybir.ActivationFunctionType
ALU = mybir.AluOpType
AXX = mybir.AxisListType.X

N_SUB = 6
NCORE = 8
D = 4096
KC = 32
TPC = 1024
NT = 8
SEQ = 4096
TB = 32
EPS = 1e-5
INTER = 11008
IC = 86
NEG0 = -3.4028235e38
MASK1 = -66504.0
L1_SCALE = 1.0 / math.sqrt(128.0)
L0_SCALE = 1.0 / 128.0

def core_tiles(c):
    return [(0, c + 8 * j) for j in range(4)] + [(1, (7 - c) + 8 * j) for j in range(4)]

GATHER = [core_tiles(r) for r in range(NCORE)]
OWNER = {}
for _r in range(NCORE):
    for _l, _bi in enumerate(GATHER[_r]):
        OWNER[_bi] = (_r, _l)

def gcol(b, i):
    r, l = OWNER[(b, i)]
    return r * TPC + l * 128

def gtile(b, i):
    r, l = OWNER[(b, i)]
    return r * NT + l

def _l0_mask():
    BLOCK, LOCAL, VERT = 64, 16, 8
    nb = SEQ // BLOCK
    qb = np.arange(nb)[:, None]; kb = np.arange(nb)[None, :]
    bm = (kb <= qb) & ((qb - kb < LOCAL) | ((kb + 1) % VERT == 0))
    em = np.kron(bm, np.ones((BLOCK, BLOCK), dtype=bool))
    ii = np.arange(SEQ)
    em &= ii[:, None] >= ii[None, :]
    att, pats, pidx = {}, [], {}
    for a in range(TB):
        lst = []
        for b in range(TB):
            t = em[a * 128:(a + 1) * 128, b * 128:(b + 1) * 128]
            if not t.any():
                continue
            if t.all():
                lst.append((b, -1))
            else:
                key = t.tobytes()
                if key not in pidx:
                    pidx[key] = len(pats)
                    pats.append(np.where(t, 0.0, NEG0).astype(np.float32))
                lst.append((b, pidx[key]))
        att[a] = lst
    return att, np.stack(pats)

L0_ATT, L0_PATS = _l0_mask()
NPAT = len(L0_PATS)

def _rope_tables(base):
    inv = base ** (-(np.arange(64, dtype=np.float64) / 64.0))
    pos = np.zeros(NCORE * TPC, dtype=np.float64)
    for r in range(NCORE):
        for l, (b, i) in enumerate(GATHER[r]):
            pos[r * TPC + l * 128:r * TPC + (l + 1) * 128] = np.arange(i * 128, (i + 1) * 128)
    ang = pos[None, :] * inv[:, None]
    cos = np.cos(ang).astype(np.float32)
    sin = np.sin(ang).astype(np.float32)
    return np.concatenate([cos, cos], 0), np.concatenate([-sin, sin], 0)

def _tile_w(w):
    K, N = w.shape
    kc, nch = K // 128, N // 128
    t = w.reshape(kc, 128, nch, 128).transpose(2, 1, 0, 3).reshape(nch, 128, K)
    return np.ascontiguousarray(t)

ENGS = ("sync", "tensor", "vector", "scalar", "gpsimd")
QSEMS = {"qw": "sync", "qx": "sync", "qo": "sync", "qa": "scalar", "qg": "gpsimd"}

class Prog:
    def __init__(self):
        self.st = {e: [] for e in ENGS}
        self.cnt = {}
        self.waited = {e: {} for e in ENGS}
        self.sems = {}
        self.guard = {}
        self.wmax = {}

    def emit(self, eng, fn):
        self.st[eng].append(fn)

    def inc(self, s, amt):
        self.cnt[s] = self.cnt.get(s, 0) + amt
        return self.cnt[s]

    def wait(self, eng, s, val):
        if not val:
            return
        if s in QSEMS:
            val = self.cnt.get(s, 0)
            if not val:
                return
        self.wmax[s] = max(self.wmax.get(s, 0), val)
        w = self.waited[eng]
        if w.get(s, 0) >= val:
            return
        w[s] = val
        sem = self.sems[s]
        self.emit(eng, lambda e, sem=sem, v=val: e.wait_ge(sem, v))

    def dma(self, q, out, in_, **kw):
        eng = QSEMS[q]
        w = self.wmax.get(q, 0)
        if w:
            ww = self.waited[eng]
            if ww.get(q, 0) < w:
                ww[q] = w
                sem0 = self.sems[q]
                self.emit(eng, lambda e, sem=sem0, v=w: e.wait_ge(sem, v))
        tgt = self.inc(q, 16)
        sem = self.sems[q]
        self.emit(eng, lambda e, o=out, i=in_, kw=kw, sem=sem:
                  e.dma_start(out=o, in_=i, **kw).then_inc(sem, 16))
        return tgt

    ESEM = {"vector": "dv", "scalar": "ac", "tensor": "pe"}

    def op(self, eng, fn, sem=None):
        if eng in ("vector", "scalar"):
            own = self.ESEM[eng]
            if sem is None:
                sem = own
            prev = self.cnt.get(own, 0)
            if prev:
                w = self.waited[eng]
                if w.get(own, 0) < prev:
                    w[own] = prev
                    s0 = self.sems[own]
                    self.emit(eng, lambda e, s0=s0, v=prev: e.wait_ge(s0, v))
        if sem is not None:
            tgt = self.inc(sem, 1)
            s = self.sems[sem]
            self.emit(eng, lambda e, fn=fn, s=s: fn(e).then_inc(s, 1))
            return tgt
        self.emit(eng, lambda e, fn=fn: fn(e))
        return None

    def claim(self, key, eng):
        for s, c in self.guard.get(key, []):
            self.wait(eng, s, c)
        self.guard[key] = []

    def claim_multi(self, key, engs):
        for s, c in self.guard.get(key, []):
            for eng in engs:
                self.wait(eng, s, c)
        self.guard[key] = []

    def consume(self, key, s, c):
        self.guard.setdefault(key, []).append((s, c))


def build_program():
    nc = bass.Bass("TRN2", target_bir_lowering=False)
    P = Prog()

    def din(name, shape, dt=F32):
        return nc.dram_tensor(name, shape, dt, kind="ExternalInput").ap()

    x0 = din("x0", [NT, 128, D])
    eye_in = din("eye", [128, 128])
    r0cos = din("r0cos", [128, NCORE * TPC]); r0sin = din("r0sin", [128, NCORE * TPC])
    r1cos = din("r1cos", [128, NCORE * TPC]); r1sin = din("r1sin", [128, NCORE * TPC])
    m0 = din("m0", [NPAT, 128, 128])
    m1 = din("m1", [128, 128])
    l0k = din("l0k", [KC, 128, D])
    l0kb = din("l0kb", [128, 32])
    gT = din("gT", [16, 8])
    lAT = din("lAT", [16, D, 32])
    lBT = din("lBT", [16, 32, D])
    lAsT = din("lAsT", [KC, 128, 32])
    lBsT = din("lBsT", [32, D])
    w0qkv = din("w0qkv", [6, 128, D])
    b0qkv = din("b0qkv", [128, 6])
    w0d = din("w0d", [KC, 128, 512])
    b0d = din("b0d", [128, 32])
    w1qkv = din("w1qkv", [12, 128, D])
    b1qkv = din("b1qkv", [128, 12])
    w1o = din("w1o", [KC, 128, 512])
    b1o = din("b1o", [128, 32])
    qwg = din("qwg", [D, 2752], I32)
    qzg = din("qzg", [32, 2752], I32)
    scg = din("scg", [32, 22016])
    qwd = din("qwd", [INTER, 512], I32)
    qzd = din("qzd", [IC, 512], I32)
    scd = din("scd", [IC, D])
    out = nc.dram_tensor("out", [NT, 128, D], F32, kind="ExternalOutput").ap()

    def dsc(name, shape, dt=F32):
        return nc.dram_tensor(name, shape, dt).ap()

    xres = dsc("xres", [NT, 128, D])
    aeffT_h = dsc("aeffT_h", [8, KC, 128, 32], BF16)
    aeffT_l = dsc("aeffT_l", [8, KC, 128, 32], BF16)
    beffT_h = dsc("beffT_h", [8, 32, D], BF16)
    beffT_l = dsc("beffT_l", [8, 32, D], BF16)
    zdq_g = dsc("zdq_g", [32, 22016])
    zdq_d = dsc("zdq_d", [IC, D])
    wgu_h = dsc("wgu_h", [172, 128, D], BF16)
    wgu_l = dsc("wgu_l", [172, 128, D], BF16)
    wdn_h = dsc("wdn_h", [KC, 128, IC * 128], BF16)
    wdn_l = dsc("wdn_l", [KC, 128, IC * 128], BF16)
    l0k_h = dsc("l0k_h", [KC, 128, D], BF16)
    l0k_l = dsc("l0k_l", [KC, 128, D], BF16)
    w0q_h = dsc("w0q_h", [6, 128, D], BF16)
    w0q_l = dsc("w0q_l", [6, 128, D], BF16)
    w0d_h = dsc("w0d_h", [KC, 128, 512], BF16)
    w0d_l = dsc("w0d_l", [KC, 128, 512], BF16)
    w1q_h = dsc("w1q_h", [12, 128, D], BF16)
    w1q_l = dsc("w1q_l", [12, 128, D], BF16)
    w1o_h = dsc("w1o_h", [KC, 128, 512], BF16)
    w1o_l = dsc("w1o_l", [KC, 128, 512], BF16)
    las_h = dsc("las_h", [KC, 128, 32], BF16)
    las_l = dsc("las_l", [KC, 128, 32], BF16)
    lbs_h = dsc("lbs_h", [32, D], BF16)
    lbs_l = dsc("lbs_l", [32, D], BF16)
    ctx_h = dsc("ctx_h", [4, 128, NCORE * TPC], BF16)
    ctx_l = dsc("ctx_l", [4, 128, NCORE * TPC], BF16)
    cc_in = dsc("cc_in", [KC, 128, TPC])
    ag_out = dsc("ag_out", [NCORE, KC, 128, TPC])
    qkvT = dsc("qkvT", [12, 128, NCORE * TPC])
    vT_tm = dsc("vT_tm", [4, NCORE * NT, 128, 128])
    rs_in = dsc("rs_in", [NCORE, KC, 128, TPC])
    rs_out = dsc("rs_out", [KC, 128, TPC])

    with ExitStack() as ctx:
        E = ctx.enter_context
        XNT = E(nc.sbuf_tensor("XNT", [128, KC * TPC], F32))
        WB = E(nc.sbuf_tensor("WB", [128, 2, D], F32))
        SC = E(nc.sbuf_tensor("SC", [128, 8192], F32))
        EYE = E(nc.sbuf_tensor("EYE", [128, 128], F32))
        BIAS = E(nc.sbuf_tensor("BIAS", [128, 128], F32))
        STAT = E(nc.sbuf_tensor("STAT", [128, 24], F32))
        MSK = E(nc.sbuf_tensor("MSK", [128, (NPAT + 1) * 128], F32))
        BEF = E(nc.sbuf_tensor("BEF", [32, 2, 9 * 128], F32))
        AEF = E(nc.sbuf_tensor("AEF", [128, 8, 32], F32))
        PTB = E(nc.sbuf_tensor("PTB", [128, 4, 128], F32))
        PS = E(nc.psum_tensor("PS", [128, 8, 512], F32))
        TMPa = lambda tb: SC[:, tb * 512:(tb + 1) * 512]
        STGa = lambda tb, q: SC[:, 1024 + tb * 512 + q * 128:1024 + tb * 512 + (q + 1) * 128]
        ROPEc = SC[:, 2048:2560]
        ROPEs = SC[:, 2560:3072]
        RSCR = SC[:, 3072:3584]
        AXB = SC[0:32, 3584:5632]

        for s in ("qw", "qx", "qo", "qa", "qg", "pe", "dv", "ac", "cc"):
            P.sems[s] = E(nc.semaphore(s))

        psb = lambda b, n=512: PS[:, b, 0:n]
        XNTB = XNT[:].bitcast(BF16)
        WBB = lambda wb: WB[:, wb, :].bitcast(BF16)
        SCB = SC[:].bitcast(BF16)

        def xh(col, w):
            return XNTB[:, col:col + w]

        def xl(col, w, base=32768):
            return XNTB[:, base + col:base + col + w]

        def mm3(bank, whi, wlo, xhi_, xlo_, first, last, sem=None, nw=512, mp=128):
            po = PS[0:mp, bank, 0:nw]
            P.op("tensor", lambda e, whi=whi, xhi_=xhi_, po=po, first=first: e.matmul(
                po, whi, xhi_, start=first, stop=False))
            P.op("tensor", lambda e, whi=whi, xlo_=xlo_, po=po: e.matmul(
                po, whi, xlo_, start=False, stop=False))
            return P.op("tensor", lambda e, wlo=wlo, xhi_=xhi_, po=po, last=last: e.matmul(
                po, wlo, xhi_, start=False, stop=last), sem=sem)

        P.dma("qx", xres, x0)
        P.dma("qx", EYE[:], eye_in)
        P.dma("qx", BIAS[:, 0:32], l0kb)
        P.dma("qx", BIAS[:, 32:38], b0qkv)
        P.dma("qx", BIAS[:, 38:70], b0d)
        P.dma("qx", BIAS[:, 70:82], b1qkv)
        t_b = P.dma("qx", BIAS[:, 82:114], b1o)
        P.dma("qx", MSK[:, 0:NPAT * 128].rearrange("p (n k) -> p n k", k=128),
              m0.rearrange("n p k -> p n k"))
        P.dma("qx", MSK[:, NPAT * 128:(NPAT + 1) * 128], m1)
        P.wait("vector", "qx", t_b)
        P.op("vector", lambda e: e.tensor_scalar(
            out=BIAS[:, 0:32], in0=BIAS[:, 0:32], scalar1=0.5, scalar2=None,
            op0=ALU.mult), sem="dv")

        GTS = SC[0:16, 0:8]
        t_g = P.dma("qx", GTS, gT)
        for which, lsrc in (("A", lAT.rearrange("n i r -> n (i r)")),
                            ("B", lBT.rearrange("n r o -> n (r o)"))):
            for ch in range(256):
                bank = ch % 2
                lb = ch % 2
                LCH = SC[0:16, 512 + lb * 512:1024 + lb * 512]
                P.claim(("lch", lb), "scalar")
                tl = P.dma("qa", LCH, lsrc[:, ch * 512:(ch + 1) * 512])
                P.claim(("ps", bank), "tensor")
                P.wait("tensor", "qx", t_g)
                P.wait("tensor", "qa", tl)
                c_mm = P.op("tensor", lambda e, b=bank, LCH=LCH: e.matmul(
                    PS[0:8, b, :], GTS, LCH, start=True, stop=True), sem="pe")
                P.consume(("lch", lb), "pe", c_mm)
                P1H = SCB[0:8, 6144 + bank * 1024:6656 + bank * 1024]
                P1L = SCB[0:8, 6656 + bank * 1024:7168 + bank * 1024]
                P.claim_multi(("p1stg", bank), ("vector", "scalar"))
                P.wait("scalar", "pe", c_mm)
                c_h = P.op("scalar", lambda e, b=bank, P1H=P1H: e.copy(
                    P1H, PS[0:8, b, :]), sem="ac")
                P.wait("vector", "ac", c_h)
                c_ev = P.op("vector", lambda e, b=bank, P1H=P1H, P1L=P1L: e.tensor_tensor(
                    out=P1L, in0=PS[0:8, b, :], in1=P1H, op=ALU.subtract), sem="dv")
                P.consume(("ps", bank), "dv", c_ev)
                P.wait("sync", "dv", c_ev)
                dh, dl = (aeffT_h, aeffT_l) if which == "A" else (beffT_h, beffT_l)
                if which == "A":
                    i0 = ch * 16
                    kcq, pq = divmod(i0, 128)
                    t_w = P.dma("qo", dh[:, kcq, pq:pq + 16, :], P1H)
                    t_w = P.dma("qo", dl[:, kcq, pq:pq + 16, :], P1L)
                else:
                    r0, o0 = divmod(ch * 512, D)
                    t_w = P.dma("qo", dh[:, r0, o0:o0 + 512], P1H)
                    t_w = P.dma("qo", dl[:, r0, o0:o0 + 512], P1L)
                P.consume(("p1stg", bank), "qo", t_w)

        P.consume(("sc",), "pe", P.cnt["pe"])
        P.consume(("sc",), "qo", P.cnt["qo"])
        P.consume(("prep",), "qo", P.cnt["qo"])
        P.consume(("prep",), "pe", P.cnt["pe"])

        def prep_one(srcap, dsth, dstl, PP, K):
            P.claim(("prep",), "sync")
            t1 = P.dma("qx", SC[0:PP, 0:K], srcap)
            fsrc = SC[0:PP, 0:K]
            hi = SCB[0:PP, 8192:8192 + K]
            lo = SCB[0:PP, 8192 + K:8192 + 2 * K]
            P.wait("scalar", "qx", t1)
            c_h = P.op("scalar", lambda e, hi=hi, fsrc=fsrc: e.copy(hi, fsrc), sem="ac")
            P.wait("vector", "ac", c_h)
            c_l = P.op("vector", lambda e, lo=lo, fsrc=fsrc, hi=hi: e.tensor_tensor(
                out=lo, in0=fsrc, in1=hi, op=ALU.subtract), sem="dv")
            P.wait("sync", "dv", c_l)
            t2 = P.dma("qo", dsth, hi)
            t2 = P.dma("qo", dstl, lo)
            P.consume(("prep",), "qo", t2)
            P.consume(("prep",), "dv", c_l)

        for srcv, dh, dl, K in ((l0k, l0k_h, l0k_l, D),
                                (w0qkv, w0q_h, w0q_l, D),
                                (w0d, w0d_h, w0d_l, 512),
                                (w1qkv, w1q_h, w1q_l, D),
                                (w1o, w1o_h, w1o_l, 512),
                                (lAsT, las_h, las_l, 32)):
            for nch in range(srcv.shape[0]):
                prep_one(srcv[nch], dh[nch], dl[nch], 128, K)
        prep_one(lBsT, lbs_h, lbs_l, 32, D)
        P.consume(("sc",), "qo", P.cnt["qo"])
        P.consume(("sc",), "dv", P.cnt["dv"])

        def unpack_z(qz, G, NPK, NN, zdst, nq):
            ZB = XNT[0:G, 0:NPK]
            t0 = P.dma("qx", ZB.bitcast(I32), qz)
            tlast = 0
            W = NN // nq
            for q in range(nq):
                ZU = XNT[0:G, 4096:4096 + W]
                ZF = XNT[0:G, 12288:12288 + W]
                P.wait("vector", "qx", t0)
                last = None
                for j in range(8):
                    last = P.op("vector", lambda e, j=j, q=q, W=W, ZU=ZU: e.tensor_scalar(
                        out=ZU.bitcast(I32).rearrange("p (a e) -> p a e", e=8)[:, :, j],
                        in0=ZB.bitcast(I32)[:, q * (W // 8):(q + 1) * (W // 8)],
                        scalar1=4 * j, scalar2=15,
                        op0=ALU.logical_shift_right, op1=ALU.bitwise_and), sem="dv")
                P.wait("scalar", "dv", last)
                c = P.op("scalar", lambda e, ZF=ZF, ZU=ZU: e.copy(ZF, ZU.bitcast(I32)), sem="ac")
                P.wait("sync", "ac", c)
                tlast = P.dma("qo", zdst[:, q * W:(q + 1) * W], ZF)
                P.wait("vector", "ac", c)
                P.wait("vector", "qo", tlast)
            P.consume(("xnt",), "qo", tlast)
            return tlast

        def dequant():
            tz_g = unpack_z(qzg, 32, 2752, 22016, zdq_g, 4)
            tz_d = unpack_z(qzd, IC, 512, D, zdq_d, 1)
            def dq(qw, GN, NN, nstr, sc_in, zdq, wdsth, wdstl, tz, tile_per_strip):
                W = NN // nstr
                PK = W // 8
                for g in range(GN):
                    for s in range(nstr):
                        ZR = XNT[:, 0:W]
                        SR = XNT[:, 5504:5504 + W]
                        UN = XNT[:, 11008:11008 + W]
                        FW = XNT[:, 16512:16512 + W]
                        qb = SC[:, 0:PK]
                        P.claim(("dqqb",), "sync")
                        t1 = P.dma("qw", qb.bitcast(I32), qw[g * 128:(g + 1) * 128, s * PK:(s + 1) * PK])
                        P.wait("scalar", "qo", tz)
                        P.claim(("dqzr",), "scalar")
                        t2 = P.dma("qa", ZR, zdq[g:g + 1, s * W:(s + 1) * W].broadcast_to([128, W]))
                        t3 = P.dma("qa", SR, sc_in[g:g + 1, s * W:(s + 1) * W].broadcast_to([128, W]))
                        P.wait("vector", "qw", t1)
                        P.claim(("dqun",), "vector")
                        last = None
                        for j in range(8):
                            last = P.op("vector", lambda e, j=j, UN=UN, qb=qb: e.tensor_scalar(
                                out=UN.bitcast(I32).rearrange("p (a e) -> p a e", e=8)[:, :, j],
                                in0=qb.bitcast(I32), scalar1=4 * j, scalar2=15,
                                op0=ALU.logical_shift_right, op1=ALU.bitwise_and), sem="dv")
                        P.consume(("dqqb",), "dv", last)
                        P.wait("scalar", "dv", last)
                        P.claim(("dqfw",), "scalar")
                        c1 = P.op("scalar", lambda e, FW=FW, UN=UN: e.copy(FW, UN.bitcast(I32)), sem="ac")
                        P.consume(("dqun",), "ac", c1)
                        P.wait("vector", "ac", c1)
                        P.wait("vector", "qa", t3)
                        c2 = P.op("vector", lambda e, FW=FW, ZR=ZR: e.tensor_tensor(
                            out=FW, in0=FW, in1=ZR, op=ALU.subtract), sem="dv")
                        c3 = P.op("vector", lambda e, FW=FW, SR=SR: e.tensor_tensor(
                            out=FW, in0=FW, in1=SR, op=ALU.mult), sem="dv")
                        P.consume(("dqzr",), "dv", c3)
                        FWH = XNTB[:, 44032:44032 + W]
                        FWL = XNTB[:, 49536:49536 + W]
                        P.wait("scalar", "dv", c3)
                        c4 = P.op("scalar", lambda e, FWH=FWH, FW=FW: e.copy(FWH, FW), sem="ac")
                        P.wait("vector", "ac", c4)
                        c5 = P.op("vector", lambda e, FWL=FWL, FW=FW, FWH=FWH: e.tensor_tensor(
                            out=FWL, in0=FW, in1=FWH, op=ALU.subtract), sem="dv")
                        P.wait("sync", "dv", c5)
                        nch0 = (s * W) // 128
                        ncnt = W // 128
                        t4 = 0
                        step = (ncnt + 3) // 4
                        for piece in range(0, ncnt, step):
                            pc = min(step, ncnt - piece)
                            for wv, fsrc in ((wdsth, FWH), (wdstl, FWL)):
                                t4 = P.dma("qo",
                                           wv[nch0 + piece:nch0 + piece + pc, :, g * 128:(g + 1) * 128]
                                           .rearrange("n p e -> p n e"),
                                           fsrc[:, piece * 128:(piece + pc) * 128]
                                           .rearrange("p (n e) -> p n e", e=128))
                        P.consume(("dqfw",), "qo", t4)
                P.consume(("xnt",), "qo", P.cnt["qo"])
                P.consume(("xnt",), "dv", P.cnt["dv"])
                P.consume(("xnt",), "ac", P.cnt["ac"])
            dq(qwg, 32, 22016, 4, scg, zdq_g, wgu_h, wgu_l, tz_g, 43)
            dq(qwd, IC, D, 1, scd, zdq_d, wdn_h, wdn_l, tz_d, 32)

        res_wm = {"qg": 0}

        def ln_phase(tiles, span, lo_base=32768):
            P.claim_multi(("xnt",), ("vector", "scalar"))
            for jj, j in enumerate(tiles):
                XT = SC[:, 0:4096]
                XN = SC[:, 4096:8192]
                P.claim(("sc",), "sync")
                P.wait("sync", "qg", res_wm["qg"])
                t1 = P.dma("qx", XT, xres[j])
                P.claim(("stat",), "vector")
                P.wait("vector", "qx", t1)
                c1 = P.op("vector", lambda e, XT=XT: e.reduce_sum(STAT[:, 0:1], XT, axis=AXX), sem="dv")
                c2 = P.op("vector", lambda e: e.tensor_scalar(
                    out=STAT[:, 1:2], in0=STAT[:, 0:1], scalar1=-1.0 / D, scalar2=None,
                    op0=ALU.mult), sem="dv")
                P.wait("scalar", "dv", c2)
                c3 = P.op("scalar", lambda e, XT=XT, XN=XN: e.activation(
                    XN, XT, AF.Square, bias=STAT[:, 1:2], scale=1.0,
                    accum_out=STAT[:, 2:3]), sem="ac")
                P.wait("vector", "ac", c3)
                c4 = P.op("vector", lambda e: e.tensor_scalar(
                    out=STAT[:, 3:4], in0=STAT[:, 2:3], scalar1=1.0 / D, scalar2=EPS,
                    op0=ALU.mult, op1=ALU.add), sem="dv")
                P.wait("scalar", "dv", c4)
                c5 = P.op("scalar", lambda e: e.activation(
                    STAT[:, 4:5], STAT[:, 3:4], AF.Sqrt), sem="ac")
                P.wait("vector", "ac", c5)
                c6 = P.op("vector", lambda e: e.reciprocal(STAT[:, 5:6], STAT[:, 4:5]), sem="dv")
                c7 = P.op("vector", lambda e: e.tensor_tensor(
                    out=STAT[:, 6:7], in0=STAT[:, 1:2], in1=STAT[:, 5:6], op=ALU.mult), sem="dv")
                P.wait("scalar", "dv", c7)
                c8 = P.op("scalar", lambda e, XT=XT, XN=XN: e.activation(
                    XN, XT, AF.Identity, bias=STAT[:, 6:7], scale=STAT[:, 5:6]), sem="ac")
                P.consume(("stat",), "ac", c8)
                lastpe = 0
                for c in range(KC):
                    bank = 4 + (c % 4)
                    P.claim(("ps", bank), "tensor")
                    P.wait("tensor", "ac", c8)
                    cm = P.op("tensor", lambda e, c=c, bank=bank, XN=XN: e.transpose(
                        PS[:, bank, 0:128], XN[:, c * 128:(c + 1) * 128], EYE[:]), sem="pe")
                    lastpe = cm
                    P.wait("scalar", "pe", cm)
                    dh = xh(c * span + jj * 128, 128)
                    dl = xl(c * span + jj * 128, 128, lo_base)
                    ch_ = P.op("scalar", lambda e, dh=dh, bank=bank: e.copy(dh, PS[:, bank, 0:128]), sem="ac")
                    P.wait("vector", "ac", ch_)
                    ce = P.op("vector", lambda e, dl=dl, dh=dh, bank=bank: e.tensor_tensor(
                        out=dl, in0=PS[:, bank, 0:128], in1=dh, op=ALU.subtract), sem="dv")
                    P.consume(("ps", bank), "dv", ce)
                P.consume(("sc",), "ac", c8)
                P.consume(("sc",), "pe", lastpe)

        def resid_write(bank, j0, nch, scale, bias_col, half):
            tb = half
            bias_ap = BIAS[:, bias_col:bias_col + 1] if bias_col is not None else 0.0
            P.claim(("tmp", tb), "scalar")
            ce = P.op("scalar", lambda e, tb=tb, bank=bank, bias_ap=bias_ap, scale=scale: e.activation(
                TMPa(tb), psb(bank), AF.Identity, bias=bias_ap, scale=scale), sem="ac")
            P.consume(("ps", bank), "ac", ce)
            lastpe = None
            for q in range(4):
                pb = 4 + (q % 2) + 2 * tb
                P.claim(("ps", pb), "tensor")
                P.wait("tensor", "ac", ce)
                cm = P.op("tensor", lambda e, q=q, pb=pb, tb=tb: e.transpose(
                    PS[:, pb, 0:128], TMPa(tb)[:, q * 128:(q + 1) * 128], EYE[:]), sem="pe")
                P.consume(("tmp", tb), "pe", cm)
                lastpe = cm
                P.claim(("stg2", tb, q), "vector")
                P.wait("vector", "pe", cm)
                cv = P.op("vector", lambda e, q=q, pb=pb, tb=tb: e.tensor_copy(
                    STGa(tb, q), PS[:, pb, 0:128]), sem="dv")
                P.consume(("ps", pb), "dv", cv)
                P.wait("gpsimd", "dv", cv)
                tq = P.dma("qg", xres[j0 + q, :, nch * 128:(nch + 1) * 128],
                           STGa(tb, q), accum_op=ALU.add)
                P.consume(("stg2", tb, q), "qg", tq)
            P.consume(("sc",), "pe", lastpe)
            P.consume(("sc",), "qg", P.cnt["qg"])
            res_wm["qg"] = P.cnt["qg"]

        AXBB = AXB.bitcast(BF16)
        AEFB = AEF[:].bitcast(BF16)
        def l0ffn():
            ln_phase(list(range(NT)), TPC)
            for j in range(NT):
                bank = j % 2
                P.claim(("ps", bank), "tensor")
                cm = None
                for kc in range(KC):
                    slot = (j * KC + kc) % 8
                    P.claim(("aef", slot), "scalar")
                    tl = P.dma("qa", AEFB[:, slot, 0:32], aeffT_h[j, kc])
                    tl = P.dma("qa", AEFB[:, slot, 32:64], aeffT_l[j, kc])
                    P.wait("tensor", "qa", tl)
                    cm = mm3(bank, AEFB[:, slot, 0:32], AEFB[:, slot, 32:64],
                             xh(kc * TPC + j * 128, 128), xl(kc * TPC + j * 128, 128),
                             kc == 0, kc == KC - 1, sem="pe", nw=128, mp=32)
                    P.consume(("aef", slot), "pe", cm)
                P.claim_multi(("axb", j), ("vector", "scalar"))
                P.wait("scalar", "pe", cm)
                c_h = P.op("scalar", lambda e, j=j, bank=bank: e.copy(
                    AXBB[:, j * 128:(j + 1) * 128], PS[0:32, bank, 0:128]), sem="ac")
                P.wait("vector", "ac", c_h)
                cv = P.op("vector", lambda e, j=j, bank=bank: e.tensor_tensor(
                    out=AXBB[:, 1024 + j * 128:1024 + (j + 1) * 128],
                    in0=PS[0:32, bank, 0:128], in1=AXBB[:, j * 128:(j + 1) * 128],
                    op=ALU.subtract), sem="dv")
                P.consume(("ps", bank), "dv", cv)
            for h in range(2):
                bank = 2 + h
                P.claim(("ps", bank), "tensor")
                for kc in range(KC):
                    slot = (NT * KC + h * KC + kc) % 8
                    P.claim(("aef", slot), "scalar")
                    tl = P.dma("qa", AEFB[:, slot, 0:32], las_h[kc])
                    tl = P.dma("qa", AEFB[:, slot, 32:64], las_l[kc])
                    P.wait("tensor", "qa", tl)
                    cm = mm3(bank, AEFB[:, slot, 0:32], AEFB[:, slot, 32:64],
                             xh(kc * TPC + h * 512, 512), xl(kc * TPC + h * 512, 512),
                             kc == 0, kc == KC - 1, sem="pe", nw=512, mp=32)
                    P.consume(("aef", slot), "pe", cm)
                P.claim_multi(("axs", h), ("vector", "scalar"))
                P.wait("scalar", "pe", cm)
                c_h = P.op("scalar", lambda e, h=h, bank=bank: e.copy(
                    AXBB[:, 2048 + h * 512:2048 + (h + 1) * 512], PS[0:32, bank, :]), sem="ac")
                P.wait("vector", "ac", c_h)
                cv = P.op("vector", lambda e, h=h, bank=bank: e.tensor_tensor(
                    out=AXBB[:, 3072 + h * 512:3072 + (h + 1) * 512],
                    in0=PS[0:32, bank, :], in1=AXBB[:, 2048 + h * 512:2048 + (h + 1) * 512],
                    op=ALU.subtract), sem="dv")
                P.consume(("ps", bank), "dv", cv)
            cax = P.cnt["dv"]
            for nch in range(KC):
                wbuf = nch % 2
                P.claim(("wb", wbuf), "sync")
                tw = P.dma("qw", WBB(wbuf)[:, 0:4096], l0k_h[nch])
                tw = P.dma("qw", WBB(wbuf)[:, 4096:8192], l0k_l[nch])
                P.claim(("bef", wbuf), "scalar")
                BEFB = BEF[:, wbuf, :].bitcast(BF16)
                P.dma("qa", BEFB[0:32, 0:1024].rearrange("r (g o) -> r g o", g=8),
                      beffT_h[:, :, nch * 128:(nch + 1) * 128].rearrange("g r o -> r g o"))
                P.dma("qa", BEFB[0:32, 1024:2048].rearrange("r (g o) -> r g o", g=8),
                      beffT_l[:, :, nch * 128:(nch + 1) * 128].rearrange("g r o -> r g o"))
                P.dma("qa", BEFB[0:32, 2048:2176], lbs_h[:, nch * 128:(nch + 1) * 128])
                tbf = P.dma("qa", BEFB[0:32, 2176:2304], lbs_l[:, nch * 128:(nch + 1) * 128])
                for h in range(2):
                    bank = 2 * (nch % 2) + h
                    P.claim(("ps", bank), "tensor")
                    P.wait("tensor", "qw", tw)
                    for kc in range(KC):
                        mm3(bank, WBB(wbuf)[:, kc * 128:(kc + 1) * 128],
                            WBB(wbuf)[:, 4096 + kc * 128:4096 + (kc + 1) * 128],
                            xh(kc * TPC + h * 512, 512), xl(kc * TPC + h * 512, 512),
                            kc == 0, False)
                    P.wait("tensor", "dv", cax)
                    P.wait("tensor", "qa", tbf)
                    BEFB2 = BEF[:, wbuf, :].bitcast(BF16)
                    for q in range(4):
                        j = h * 4 + q
                        for whi, wlo, xhi_, xlo_ in ((BEFB2[0:32, j * 128:(j + 1) * 128],
                                                      BEFB2[0:32, 1024 + j * 128:1024 + (j + 1) * 128],
                                                      AXBB[0:32, j * 128:(j + 1) * 128],
                                                      AXBB[0:32, 1024 + j * 128:1024 + (j + 1) * 128]),):
                            P.op("tensor", lambda e, whi=whi, xhi_=xhi_, bank=bank, q=q: e.matmul(
                                PS[:, bank, q * 128:(q + 1) * 128], whi, xhi_, start=False, stop=False))
                            P.op("tensor", lambda e, whi=whi, xlo_=xlo_, bank=bank, q=q: e.matmul(
                                PS[:, bank, q * 128:(q + 1) * 128], whi, xlo_, start=False, stop=False))
                            P.op("tensor", lambda e, wlo=wlo, xhi_=xhi_, bank=bank, q=q: e.matmul(
                                PS[:, bank, q * 128:(q + 1) * 128], wlo, xhi_, start=False, stop=False))
                    P.op("tensor", lambda e, bank=bank, wbuf=wbuf, h=h: e.matmul(
                        psb(bank), BEF[:, wbuf, :].bitcast(BF16)[0:32, 2048:2176],
                        AXBB[0:32, 2048 + h * 512:2048 + h * 512 + 512],
                        start=False, stop=False))
                    P.op("tensor", lambda e, bank=bank, wbuf=wbuf, h=h: e.matmul(
                        psb(bank), BEF[:, wbuf, :].bitcast(BF16)[0:32, 2048:2176],
                        AXBB[0:32, 3072 + h * 512:3072 + h * 512 + 512],
                        start=False, stop=False))
                    cm = P.op("tensor", lambda e, bank=bank, wbuf=wbuf, h=h: e.matmul(
                        psb(bank), BEF[:, wbuf, :].bitcast(BF16)[0:32, 2176:2304],
                        AXBB[0:32, 2048 + h * 512:2048 + h * 512 + 512],
                        start=False, stop=True), sem="pe")
                    P.consume(("wb", wbuf), "pe", cm)
                    P.consume(("bef", wbuf), "pe", cm)
                    P.consume(("sc",), "pe", cm)
                    for j in range(h * 4, h * 4 + 4):
                        P.consume(("axb", j), "pe", cm)
                    P.consume(("axs", h), "pe", cm)
                    P.wait("scalar", "pe", cm)
                    resid_write(bank, h * 4, nch, 0.5, nch, h)

        def attn(layer):
            if layer == 0:
                nch_tot, rq, rk = 6, [0, 1, 2, 3], [4]
                vch = {5: 0}
                wqh, wql, woh, wol = w0q_h, w0q_l, w0d_h, w0d_l
                bq_col, bo_col = 32, 38
                rcos, rsin = r0cos, r0sin
                scale = L0_SCALE
                heads = [(b, 0) for b in range(2)]
                qof = lambda hl, qh: qh
                kof = lambda hl: 4
                vof = lambda hl: 5 if False else 0
                nqh = 4
                att_of = lambda i: L0_ATT[i]
            else:
                nch_tot, rq, rk = 12, [0, 1, 2, 3], [4, 5, 6, 7]
                vch = {8: 0, 9: 1, 10: 2, 11: 3}
                wqh, wql, woh, wol = w1q_h, w1q_l, w1o_h, w1o_l
                bq_col, bo_col = 70, 82
                rcos, rsin = r1cos, r1sin
                scale = L1_SCALE
                heads = [(b, hl) for b in range(2) for hl in range(4)]
                nqh = 1
                att_of = lambda i: [(kt, (-1 if kt < i else NPAT)) for kt in range(i + 1)]

            ln_phase(list(range(NT)), TPC)
            dvE, acE = P.cnt["dv"], P.cnt["ac"]
            P.wait("sync", "dv", dvE)
            P.wait("sync", "ac", acE)
            t_ci = 0
            P.claim(("ccin",), "sync")
            for kq in range(0, KC, 8):
                t_ci = P.dma("qo", cc_in[kq:kq + 8].rearrange("kc p t -> p kc t"),
                             XNT[:, kq * TPC:(kq + 8) * TPC].rearrange("p (kc t) -> p kc t", t=TPC))
            P.consume(("xnt",), "qo", t_ci)
            P.wait("gpsimd", "qo", t_ci)
            P.claim(("agout",), "gpsimd")
            c_ag = P.op("gpsimd", lambda e: e.collective_compute(
                "AllGather", ALU.bypass, replica_groups=[list(range(NCORE))],
                ins=[cc_in.opt()], outs=[ag_out.opt()]), sem="cc")
            P.consume(("ccin",), "cc", c_ag)

            QA = lambda par: SC[:, par * 512:(par + 1) * 512]
            QR = lambda par: SC[:, 1024 + par * 512:1536 + par * 512]
            RC = SC[:, 2048:2560]
            RS = SC[:, 2560:3072]
            VST = lambda par: SC[:, 3072 + par * 512:3584 + par * 512]
            WS = lambda par: SC[:, 4096 + par * 512:4608 + par * 512]
            P.claim_multi(("sc",), ("scalar", "vector", "sync"))
            uc = 0
            for blk in range(NCORE):
                P.claim(("xnt",), "sync")
                P.wait("sync", "cc", c_ag)
                tg = 0
                for kq in range(0, KC, 8):
                    tg = P.dma("qx", XNT[:, kq * TPC:(kq + 8) * TPC].rearrange("p (kc t) -> p kc t", t=TPC),
                               ag_out[blk, kq:kq + 8].rearrange("kc p t -> p kc t"))
                for nch in range(nch_tot):
                    wb = nch % 2
                    P.claim(("wb", wb), "sync")
                    tw = P.dma("qw", WBB(wb)[:, 0:4096], wqh[nch])
                    tw = P.dma("qw", WBB(wb)[:, 4096:8192], wql[nch])
                    for tb in range(2):
                        bank = 2 * (nch % 2) + tb
                        P.claim(("ps", bank), "tensor")
                        P.wait("tensor", "qw", tw)
                        P.wait("tensor", "qx", tg)
                        cm = None
                        for kc in range(KC):
                            cm = mm3(bank, WBB(wb)[:, kc * 128:(kc + 1) * 128],
                                     WBB(wb)[:, 4096 + kc * 128:4096 + (kc + 1) * 128],
                                     xh(kc * TPC + tb * 512, 512), xl(kc * TPC + tb * 512, 512),
                                     kc == 0, kc == KC - 1,
                                     sem=("pe" if kc == KC - 1 else None))
                        P.consume(("wb", wb), "pe", cm)
                        P.consume(("xnt",), "pe", cm)
                        tok0 = blk * TPC + tb * 512
                        par = uc % 2
                        uc += 1
                        if nch in rq or nch in rk:
                            P.claim(("rc",), "scalar")
                            trc = P.dma("qa", RC, rcos[:, tok0:tok0 + 512])
                            trs = P.dma("qa", RS, rsin[:, tok0:tok0 + 512])
                            P.claim(("qa", par), "scalar")
                            P.wait("scalar", "pe", cm)
                            ce = P.op("scalar", lambda e, par=par, bank=bank, nch=nch: e.activation(
                                QA(par), psb(bank), AF.Identity,
                                bias=BIAS[:, bq_col + nch:bq_col + nch + 1], scale=1.0), sem="ac")
                            P.consume(("ps", bank), "ac", ce)
                            P.claim(("qr", par), "scalar")
                            P.wait("scalar", "ac", ce)
                            tr1 = P.dma("qa", QR(par)[0:64, :], QA(par)[64:128, :])
                            tr2 = P.dma("qa", QR(par)[64:128, :], QA(par)[0:64, :])
                            P.wait("vector", "qa", tr2)
                            P.wait("vector", "ac", ce)
                            c1 = P.op("vector", lambda e, par=par: e.tensor_tensor(
                                out=QA(par), in0=QA(par), in1=RC, op=ALU.mult), sem="dv")
                            c2 = P.op("vector", lambda e, par=par: e.tensor_tensor(
                                out=QR(par), in0=QR(par), in1=RS, op=ALU.mult), sem="dv")
                            P.consume(("rc",), "dv", c2)
                            c3 = P.op("vector", lambda e, par=par: e.tensor_tensor(
                                out=QA(par), in0=QA(par), in1=QR(par), op=ALU.add), sem="dv")
                            P.consume(("qr", par), "dv", c3)
                            P.wait("sync", "dv", c3)
                            t_o = P.dma("qo", qkvT[nch, :, tok0:tok0 + 512], QA(par))
                            P.consume(("qa", par), "qo", t_o)
                            P.consume(("qa", par), "dv", c3)
                        elif nch in vch:
                            vl = vch[nch]
                            P.claim(("vst", par), "scalar")
                            P.wait("scalar", "pe", cm)
                            ce = P.op("scalar", lambda e, par=par, bank=bank, nch=nch: e.activation(
                                VST(par), psb(bank), AF.Identity,
                                bias=BIAS[:, bq_col + nch:bq_col + nch + 1], scale=1.0), sem="ac")
                            P.consume(("ps", bank), "ac", ce)
                            for q in range(4):
                                pb = 4 + (q % 2) + 2 * (tb % 2)
                                P.claim(("ps", pb), "tensor")
                                P.wait("tensor", "ac", ce)
                                cmq = P.op("tensor", lambda e, q=q, pb=pb, par=par: e.transpose(
                                    PS[:, pb, 0:128], VST(par)[:, q * 128:(q + 1) * 128], EYE[:]), sem="pe")
                                P.consume(("vst", par), "pe", cmq)
                                wsp = q % 2
                                P.claim(("ws", wsp), "vector")
                                P.wait("vector", "pe", cmq)
                                cv = P.op("vector", lambda e, q=q, pb=pb, wsp=wsp: e.tensor_copy(
                                    WS(wsp)[:, 0:128] if False else SC[:, 4096 + wsp * 512:4096 + wsp * 512 + 128],
                                    PS[:, pb, 0:128]), sem="dv")
                                P.consume(("ps", pb), "dv", cv)
                                gt = blk * NT + tb * 4 + q
                                P.wait("sync", "dv", cv)
                                t_o = P.dma("qo", vT_tm[vl, gt], SC[:, 4096 + wsp * 512:4096 + wsp * 512 + 128])
                                P.consume(("ws", wsp), "qo", t_o)
                        else:
                            raise AssertionError("unmapped chunk")
            P.consume(("sc",), "dv", P.cnt["dv"])
            P.consume(("sc",), "ac", P.cnt["ac"])
            P.consume(("sc",), "qo", P.cnt["qo"])
            P.consume(("sc",), "qa", P.cnt["qa"])

            QTA = lambda pr: XNT[:, pr * 20480:pr * 20480 + 4096]
            KTA = lambda pr: XNT[:, 4096 + pr * 20480:8192 + pr * 20480]
            VBA = lambda pr: XNT[:, 8192 + pr * 20480:12288 + pr * 20480]
            SCOR2 = lambda par: XNT[:, 12288 + par * 4096:16384 + par * 4096]

            def emit_kv_loads(pi, q):
                b, hl = heads[pi]
                pr = pi % 2 if layer == 1 else 0
                kch = 4 + (0 if layer == 0 else hl)
                vl = 0 if layer == 0 else hl
                P.wait(QSEMS[q], "qo", 1)
                P.claim(("xk", pr), QSEMS[q])
                P.claim(("xv", pr), QSEMS[q])
                t = 0
                for i in range(TB):
                    t = P.dma(q, KTA(pr)[:, i * 128:(i + 1) * 128],
                              qkvT[kch, :, gcol(b, i):gcol(b, i) + 128])
                    t = P.dma(q, VBA(pr)[:, i * 128:(i + 1) * 128], vT_tm[vl, gtile(b, i)])
                if layer == 1:
                    emit_q_loads(pi, 0, q)
                return t

            def emit_q_loads(pi, qh, q):
                b, hl = heads[pi]
                pr = pi % 2 if layer == 1 else qh % 2
                qch = qh if layer == 0 else hl
                P.wait(QSEMS[q], "qo", 1)
                P.claim(("xq", pr), QSEMS[q])
                t = 0
                for i in range(TB):
                    t = P.dma(q, QTA(pr)[:, i * 128:(i + 1) * 128],
                              qkvT[qch, :, gcol(b, i):gcol(b, i) + 128])
                return t

            def emit_scores(pi, qh, qt):
                b, hl = heads[pi]
                pr = pi % 2 if layer == 1 else qh % 2
                kpr = pi % 2 if layer == 1 else 0
                par = qt % 2
                lst = att_of(qt)
                runs, run = [], [lst[0]]
                for kt, pat in lst[1:]:
                    if kt == run[-1][0] + 1 and len(run) < 4:
                        run.append((kt, pat))
                    else:
                        runs.append(run); run = [(kt, pat)]
                runs.append(run)
                P.claim_multi(("scor", par), ("scalar", "vector"))
                ci = 0
                for rb, run in enumerate(runs):
                    bank = rb % 2
                    k0 = run[0][0]
                    w = len(run) * 128
                    P.claim(("ps", bank), "tensor")
                    P.wait("tensor", "qx", 1)
                    P.wait("tensor", "qg", 1)
                    cm = P.op("tensor", lambda e, qt=qt, k0=k0, w=w, bank=bank, pr=pr, kpr=kpr: e.matmul(
                        PS[:, bank, 0:w], QTA(pr)[:, qt * 128:(qt + 1) * 128],
                        KTA(kpr)[:, k0 * 128:k0 * 128 + w], start=True, stop=True), sem="pe")
                    P.consume(("xq", pr), "pe", cm)
                    P.consume(("xk", kpr), "pe", cm)
                    if all(p < 0 for _, p in run):
                        P.wait("scalar", "pe", cm)
                        ce = P.op("scalar", lambda e, ci=ci, w=w, bank=bank, par=par: e.copy(
                            SCOR2(par)[:, ci * 128:ci * 128 + w], PS[:, bank, 0:w]), sem="ac")
                        P.consume(("ps", bank), "ac", ce)
                    else:
                        P.wait("vector", "pe", cm)
                        ce = None
                        for x_, (kt, pat) in enumerate(run):
                            if pat < 0:
                                ce = P.op("vector", lambda e, ci=ci, x_=x_, bank=bank, par=par: e.tensor_copy(
                                    SCOR2(par)[:, (ci + x_) * 128:(ci + x_ + 1) * 128],
                                    PS[:, bank, x_ * 128:(x_ + 1) * 128]), sem="dv")
                            else:
                                ce = P.op("vector", lambda e, ci=ci, x_=x_, bank=bank, pat=pat, par=par: e.tensor_tensor(
                                    out=SCOR2(par)[:, (ci + x_) * 128:(ci + x_ + 1) * 128],
                                    in0=PS[:, bank, x_ * 128:(x_ + 1) * 128],
                                    in1=MSK[:, pat * 128:(pat + 1) * 128], op=ALU.add), sem="dv")
                        P.consume(("ps", bank), "dv", ce)
                    ci += len(run)
                return lst

            def emit_rest(pi, qh, qt, lst):
                b, hl = heads[pi]
                pr = pi % 2 if layer == 1 else 0
                par = qt % 2
                so = 8 + 8 * par
                natt = len(lst)
                c1 = P.op("vector", lambda e, natt=natt, par=par, so=so: e.reduce_max(
                    STAT[:, so:so + 1], SCOR2(par)[:, 0:natt * 128], axis=AXX), sem="dv")
                c2 = P.op("vector", lambda e, so=so: e.tensor_scalar(
                    out=STAT[:, so + 1:so + 2], in0=STAT[:, so:so + 1], scalar1=-scale,
                    scalar2=None, op0=ALU.mult), sem="dv")
                P.wait("scalar", "dv", c2)
                c3 = P.op("scalar", lambda e, natt=natt, par=par, so=so: e.activation(
                    SCOR2(par)[:, 0:natt * 128], SCOR2(par)[:, 0:natt * 128], AF.Exp,
                    bias=STAT[:, so + 1:so + 2], scale=scale,
                    accum_out=STAT[:, so + 2:so + 3]), sem="ac")
                P.wait("vector", "ac", c3)
                c4 = P.op("vector", lambda e, so=so: e.reciprocal(
                    STAT[:, so + 3:so + 4], STAT[:, so + 2:so + 3]), sem="dv")
                P.wait("scalar", "dv", c4)
                c5 = P.op("scalar", lambda e, natt=natt, par=par, so=so: e.activation(
                    SCOR2(par)[:, 0:natt * 128], SCOR2(par)[:, 0:natt * 128], AF.Identity,
                    scale=STAT[:, so + 3:so + 4]), sem="ac")
                cb = 2 + par
                P.claim(("ps", cb), "tensor")
                lastmm = None
                lastt = None
                for c_, (kt, pat) in enumerate(lst):
                    pb = 4 + (c_ % 4)
                    P.claim(("ps", pb), "tensor")
                    P.wait("tensor", "ac", c5)
                    cmt = P.op("tensor", lambda e, c_=c_, pb=pb, par=par: e.transpose(
                        PS[:, pb, 0:128], SCOR2(par)[:, c_ * 128:(c_ + 1) * 128], EYE[:]), sem="pe")
                    lastt = cmt
                    sl = c_ % 4
                    P.claim(("ptb", sl), "vector")
                    P.wait("vector", "pe", cmt)
                    cv = P.op("vector", lambda e, sl=sl, pb=pb: e.tensor_copy(
                        PTB[:, sl, :], PS[:, pb, 0:128]), sem="dv")
                    P.consume(("ps", pb), "dv", cv)
                    P.wait("tensor", "dv", cv)
                    lastmm = P.op("tensor", lambda e, kt=kt, sl=sl, cb=cb, c_=c_, natt=natt, pr=pr: e.matmul(
                        PS[:, cb, 0:128], VBA(pr)[:, kt * 128:(kt + 1) * 128],
                        PTB[:, sl, :], start=(c_ == 0), stop=(c_ == natt - 1)), sem="pe")
                    P.consume(("ptb", sl), "pe", lastmm)
                P.consume(("xv", pr), "pe", lastmm)
                P.consume(("scor", par), "pe", lastt)
                P.consume(("scor", par), "ac", c5)
                wsp = par
                chi = SCB[:, 10240 + wsp * 1024:10240 + wsp * 1024 + 128]
                clo = SCB[:, 10240 + wsp * 1024 + 128:10240 + wsp * 1024 + 256]
                P.claim_multi(("ws2", wsp), ("vector", "scalar"))
                P.wait("scalar", "pe", lastmm)
                c_h = P.op("scalar", lambda e, chi=chi, cb=cb: e.copy(
                    chi, PS[:, cb, 0:128]), sem="ac")
                P.wait("vector", "ac", c_h)
                cv = P.op("vector", lambda e, clo=clo, chi=chi, cb=cb: e.tensor_tensor(
                    out=clo, in0=PS[:, cb, 0:128], in1=chi, op=ALU.subtract), sem="dv")
                P.consume(("ps", cb), "dv", cv)
                chl = (hl * nqh + qh) if layer == 0 else hl
                P.wait("sync", "dv", cv)
                t_o = P.dma("qo", ctx_h[chl, :, gcol(b, qt):gcol(b, qt) + 128], chi)
                t_o = P.dma("qo", ctx_l[chl, :, gcol(b, qt):gcol(b, qt) + 128], clo)
                P.consume(("ws2", wsp), "qo", t_o)
                P.consume(("sc",), "qo", t_o)

            P.claim(("xnt",), "sync")
            prev = None
            for pi in range(len(heads)):
                if pi == 0 or layer == 0:
                    if prev is not None:
                        emit_rest(*prev)
                        prev = None
                    emit_kv_loads(pi, "qx")
                    if layer == 0:
                        emit_q_loads(pi, 0, "qx")
                for qh in range(nqh):
                    for qt in range(TB):
                        lst = emit_scores(pi, qh, qt)
                        if prev is not None:
                            emit_rest(*prev)
                        prev = (pi, qh, qt, lst)
                        if qh == 0 and qt == 0 and layer == 1 and pi + 1 < len(heads):
                            emit_kv_loads(pi + 1, "qg")
                        if layer == 0 and qt == 0 and qh + 1 < nqh:
                            emit_q_loads(pi, qh + 1, "qg")
            emit_rest(*prev)
            for pr in (0, 1):
                P.consume(("xq", pr), "pe", P.cnt["pe"])
                P.consume(("xk", pr), "pe", P.cnt["pe"])
                P.consume(("xv", pr), "pe", P.cnt["pe"])
            for par in (0, 1):
                P.consume(("scor", par), "pe", P.cnt["pe"])
            P.consume(("xnt",), "pe", P.cnt["pe"])

            P.claim(("xnt",), "sync")
            P.wait("sync", "qo", P.cnt["qo"])
            tg = 0
            for chl in range(4):
                tg = P.dma("qx", XNTB[:, chl * 8192:(chl + 1) * 8192], ctx_h[chl])
                tg = P.dma("qx", XNTB[:, 32768 + chl * 8192:32768 + (chl + 1) * 8192], ctx_l[chl])
            P.consume(("ctx",), "qx", tg)
            for nch in range(KC):
                wb = nch % 2
                P.claim(("wb", wb), "sync")
                tw = P.dma("qw", WBB(wb)[:, 0:512], woh[nch])
                tw = P.dma("qw", WBB(wb)[:, 512:1024], wol[nch])
                for tbk in range(16):
                    bank = 2 * (nch % 2) + tbk % 2
                    P.claim(("ps", bank), "tensor")
                    P.wait("tensor", "qw", tw)
                    P.wait("tensor", "qx", tg)
                    cm = None
                    for kc in range(4):
                        cm = mm3(bank, WBB(wb)[:, kc * 128:(kc + 1) * 128],
                                 WBB(wb)[:, 512 + kc * 128:512 + (kc + 1) * 128],
                                 xh(kc * 8192 + tbk * 512, 512), xl(kc * 8192 + tbk * 512, 512),
                                 kc == 0, kc == 3,
                                 sem=("pe" if kc == 3 else None))
                    P.consume(("wb", wb), "pe", cm)
                    P.consume(("xnt",), "pe", cm)
                    par = tbk % 2
                    P.claim(("ops", par), "vector")
                    P.wait("vector", "pe", cm)
                    cv = P.op("vector", lambda e, par=par, bank=bank: e.tensor_copy(
                        SC[:, 5120 + par * 512:5120 + par * 512 + 512], psb(bank)), sem="dv")
                    P.consume(("ps", bank), "dv", cv)
                    P.wait("sync", "dv", cv)
                    t_o = P.dma("qo", rs_in[tbk // 2, nch, :, (tbk % 2) * 512:(tbk % 2) * 512 + 512],
                                SC[:, 5120 + par * 512:5120 + par * 512 + 512])
                    P.consume(("ops", par), "qo", t_o)
                    P.consume(("sc",), "qo", t_o)
            P.wait("gpsimd", "qo", P.cnt["qo"])
            P.claim(("rso",), "gpsimd")
            c_rs = P.op("gpsimd", lambda e: e.collective_compute(
                "ReduceScatter", ALU.add, replica_groups=[list(range(NCORE))],
                ins=[rs_in.opt()], outs=[rs_out.opt()]), sem="cc")
            for nch in range(KC):
                par = nch % 2
                P.claim(("rsb", par), "sync")
                P.wait("sync", "cc", c_rs)
                tr = P.dma("qx", SC[:, par * 1024:(par + 1) * 1024], rs_out[nch])
                P.claim(("rsc", par), "scalar")
                P.wait("scalar", "qx", tr)
                ce = P.op("scalar", lambda e, par=par, nch=nch: e.activation(
                    SC[:, 2048 + par * 1024:2048 + (par + 1) * 1024],
                    SC[:, par * 1024:(par + 1) * 1024], AF.Identity,
                    bias=BIAS[:, bo_col + nch:bo_col + nch + 1], scale=1.0), sem="ac")
                P.consume(("rsb", par), "ac", ce)
                for j in range(NT):
                    pb = 4 + (j % 4)
                    P.claim(("ps", pb), "tensor")
                    P.wait("tensor", "ac", ce)
                    cmt = P.op("tensor", lambda e, j=j, pb=pb, par=par: e.transpose(
                        PS[:, pb, 0:128],
                        SC[:, 2048 + par * 1024 + j * 128:2048 + par * 1024 + (j + 1) * 128],
                        EYE[:]), sem="pe")
                    P.consume(("rsc", par), "pe", cmt)
                    wsp = j % 2
                    P.claim(("rst", par, wsp), "vector")
                    P.wait("vector", "pe", cmt)
                    cv = P.op("vector", lambda e, pb=pb, par=par, wsp=wsp: e.tensor_copy(
                        SC[:, 4096 + (2 * par + wsp) * 128:4096 + (2 * par + wsp) * 128 + 128],
                        PS[:, pb, 0:128]), sem="dv")
                    P.consume(("ps", pb), "dv", cv)
                    P.wait("gpsimd", "dv", cv)
                    tq_ = P.dma("qg", xres[j, :, nch * 128:(nch + 1) * 128],
                                SC[:, 4096 + (2 * par + wsp) * 128:4096 + (2 * par + wsp) * 128 + 128],
                                accum_op=ALU.add)
                    P.consume(("rst", par, wsp), "qg", tq_)
                P.consume(("sc",), "pe", P.cnt["pe"])
                P.consume(("sc",), "qg", P.cnt["qg"])
            res_wm["qg"] = P.cnt["qg"]
            P.consume(("ctx",), "pe", P.cnt["pe"])
            P.consume(("rso",), "qx", P.cnt["qx"])


        def l1ffn():
            P.claim(("wb", 0), "sync")
            P.claim(("wb", 1), "sync")
            for h in range(2):
                tiles = [4 * h + q for q in range(4)]
                ln_phase(tiles, 512, lo_base=16384)
                OUT2 = lambda dch: XNT[:, 16384 + dch * 512:16384 + (dch + 1) * 512]
                HGH = lambda gi: SCB[:, 8192 + gi * 1024:8192 + gi * 1024 + 512]
                HGL = lambda gi: SCB[:, 8192 + gi * 1024 + 512:8192 + (gi + 1) * 1024]
                ngrp = (IC + 7) // 8
                wslot = [0]
                def wb_load(viewh, viewl, width):
                    s = wslot[0] % 4
                    wslot[0] += 1
                    base = WB[:, s // 2, (s % 2) * 2048:(s % 2 + 1) * 2048].bitcast(BF16)
                    P.claim(("wb8", s), "sync")
                    P.dma("qw", base[:, 0:width], viewh)
                    t = P.dma("qw", base[:, width:2 * width], viewl)
                    return base, s, t
                for gg in range(ngrp):
                    ilist = list(range(gg * 8, min(IC, gg * 8 + 8)))
                    for gi, i in enumerate(ilist):
                        gb = (2 * i) % 4
                        ub = (2 * i + 1) % 4
                        cmg = cmu = None
                        for part, bank in ((0, gb), (1, ub)):
                            nch = i if part == 0 else 86 + i
                            P.claim(("ps", bank), "tensor")
                            for kh in range(2):
                                base, s, tw = wb_load(wgu_h[nch, :, kh * 2048:(kh + 1) * 2048],
                                                      wgu_l[nch, :, kh * 2048:(kh + 1) * 2048], 2048)
                                P.wait("tensor", "qw", tw)
                                cm = None
                                for kk in range(16):
                                    kc = kh * 16 + kk
                                    cm = mm3(bank, base[:, kk * 128:(kk + 1) * 128],
                                             base[:, 2048 + kk * 128:2048 + (kk + 1) * 128],
                                             xh(kc * 512, 512), xl(kc * 512, 512, 16384),
                                             kc == 0, kc == KC - 1,
                                             sem=("pe" if (kc == KC - 1 or kk == 15) else None))
                                P.consume(("wb8", s), "pe", cm)
                            if part == 0:
                                cmg = cm
                            else:
                                cmu = cm
                        par = i % 2
                        P.claim(("hgs", par), "scalar")
                        P.wait("scalar", "pe", cmg)
                        cs = P.op("scalar", lambda e, par=par, gb=gb: e.activation(
                            SC[:, par * 512:(par + 1) * 512], psb(gb), AF.Silu), sem="ac")
                        P.consume(("ps", gb), "ac", cs)
                        P.claim(("hft", par), "vector")
                        P.wait("vector", "ac", cs)
                        P.wait("vector", "pe", cmu)
                        cv = P.op("vector", lambda e, par=par, ub=ub: e.tensor_tensor(
                            out=SC[:, 1024 + par * 512:1024 + (par + 1) * 512],
                            in0=SC[:, par * 512:(par + 1) * 512],
                            in1=psb(ub), op=ALU.mult), sem="dv")
                        P.consume(("ps", ub), "dv", cv)
                        P.consume(("hgs", par), "dv", cv)
                        P.claim(("hg", gi), "scalar")
                        P.wait("scalar", "dv", cv)
                        c_h = P.op("scalar", lambda e, gi=gi, par=par: e.copy(
                            HGH(gi), SC[:, 1024 + par * 512:1024 + (par + 1) * 512]), sem="ac")
                        P.wait("vector", "ac", c_h)
                        c_l = P.op("vector", lambda e, gi=gi, par=par: e.tensor_tensor(
                            out=HGL(gi), in0=SC[:, 1024 + par * 512:1024 + (par + 1) * 512],
                            in1=HGH(gi), op=ALU.subtract), sem="dv")
                        P.consume(("hft", par), "dv", c_l)
                        P.consume(("hg", gi), "dv", c_l) if False else None
                        P.consume(("sc",), "dv", c_l)
                    cv_h = P.cnt["dv"]
                    for dch in range(KC):
                        bank = dch % 2
                        P.claim(("ps", bank), "tensor")
                        wdw = len(ilist) * 128
                        base, s, tw = wb_load(wdn_h[dch, :, ilist[0] * 128:ilist[0] * 128 + wdw],
                                              wdn_l[dch, :, ilist[0] * 128:ilist[0] * 128 + wdw], wdw)
                        P.wait("tensor", "qw", tw)
                        P.wait("tensor", "dv", cv_h)
                        cm = None
                        for gi in range(len(ilist)):
                            cm = mm3(bank, base[:, gi * 128:(gi + 1) * 128],
                                     base[:, wdw + gi * 128:wdw + (gi + 1) * 128],
                                     HGH(gi), HGL(gi),
                                     gi == 0, gi == len(ilist) - 1,
                                     sem=("pe" if gi == len(ilist) - 1 else None))
                        P.consume(("wb8", s), "pe", cm)
                        for gi in range(len(ilist)):
                            P.consume(("hg", gi), "pe", cm)
                        P.claim(("o2", dch), "vector")
                        P.wait("vector", "pe", cm)
                        if gg == 0:
                            ce = P.op("vector", lambda e, dch=dch, bank=bank, OUT2=OUT2: e.tensor_copy(
                                OUT2(dch), psb(bank)), sem="dv")
                        else:
                            ce = P.op("vector", lambda e, dch=dch, bank=bank, OUT2=OUT2: e.tensor_tensor(
                                out=OUT2(dch), in0=OUT2(dch), in1=psb(bank), op=ALU.add), sem="dv")
                        P.consume(("ps", bank), "dv", ce)
                        P.consume(("xnt",), "dv", ce)
                        P.consume(("sc",), "pe", cm)
                for dch in range(KC):
                    for q in range(4):
                        pb = 4 + ((dch * 4 + q) % 4)
                        P.claim(("ps", pb), "tensor")
                        P.wait("tensor", "dv", P.cnt["dv"])
                        cmt = P.op("tensor", lambda e, dch=dch, q=q, pb=pb, OUT2=OUT2: e.transpose(
                            PS[:, pb, 0:128], OUT2(dch)[:, q * 128:(q + 1) * 128], EYE[:]), sem="pe")
                        P.consume(("xnt",), "pe", cmt)
                        wsp = (dch * 4 + q) % 4
                        P.claim(("l1s", wsp), "vector")
                        P.wait("vector", "pe", cmt)
                        cv = P.op("vector", lambda e, pb=pb, wsp=wsp: e.tensor_scalar(
                            out=SC[:, wsp * 128:(wsp + 1) * 128], in0=PS[:, pb, 0:128],
                            scalar1=0.5, scalar2=None, op0=ALU.mult), sem="dv")
                        P.consume(("ps", pb), "dv", cv)
                        P.wait("gpsimd", "dv", cv)
                        tq_ = P.dma("qg", xres[tiles[q], :, dch * 128:(dch + 1) * 128],
                                    SC[:, wsp * 128:(wsp + 1) * 128], accum_op=ALU.add)
                        P.consume(("l1s", wsp), "qg", tq_)
                        P.consume(("sc",), "qg", tq_)
                res_wm["qg"] = P.cnt["qg"]
            P.consume(("wb", 0), "pe", P.cnt["pe"])
            P.consume(("wb", 1), "pe", P.cnt["pe"])

        if N_SUB >= 4:
            dequant()
        if N_SUB >= 1:
            l0ffn()
        if N_SUB >= 2:
            attn(0)
        if N_SUB >= 3:
            l0ffn()
        if N_SUB >= 4:
            l1ffn()
        if N_SUB >= 5:
            attn(1)
        if N_SUB >= 6:
            l1ffn()

        P.wait("sync", "qg", res_wm["qg"])
        tout = P.dma("qo", out, xres)
        P.wait("sync", "qo", tout)

        with nc.Block() as block:
            block.sync(lambda e: [f(e) for f in P.st["sync"]])
            block.tensor(lambda e: [f(e) for f in P.st["tensor"]])
            block.vector(lambda e: [f(e) for f in P.st["vector"]])
            block.scalar(lambda e: [f(e) for f in P.st["scalar"]])
            block.gpsimd(lambda e: [f(e) for f in P.st["gpsimd"]])
    return nc


def kernel(**ins):
    x = np.ascontiguousarray(ins["L0_attn_hidden_states"], dtype=np.float32)
    xt = x.reshape(2, TB, 128, D)
    eye = np.eye(128, dtype=np.float32)
    r0c, r0s = _rope_tables(1000000.0)
    r1c, r1s = _rope_tables(10000.0)
    m1 = np.triu(np.full((128, 128), MASK1 / L1_SCALE, dtype=np.float32), 1)

    l0k_t = _tile_w(np.ascontiguousarray(ins["L0_ffn_kernel"], np.float32))
    w0qkvT = np.ascontiguousarray(ins["L0_attn_w_qkv"], np.float32).T
    w0dT = np.ascontiguousarray(ins["L0_attn_w_dense"], np.float32).T
    w1qkv = np.ascontiguousarray(ins["L1_attn_qkv_w"], np.float32)
    w1o_full = np.ascontiguousarray(ins["L1_attn_out_w"], np.float32)

    lAT = np.ascontiguousarray(ins["L0_ffn_lora_A"].transpose(0, 2, 1), np.float32)
    lBT = np.ascontiguousarray(ins["L0_ffn_lora_B"].transpose(0, 2, 1), np.float32)
    lAsT = np.ascontiguousarray(np.asarray(ins["L0_ffn_lora_A_shared"], np.float32).T
                                .reshape(KC, 128, 32))
    lBsT = np.ascontiguousarray(np.asarray(ins["L0_ffn_lora_B_shared"], np.float32).T)
    gT = np.ascontiguousarray(np.asarray(ins["L0_ffn_gate"], np.float32).T)

    Emb = D
    in_maps = []
    for c in range(NCORE):
        w0s = _tile_w(np.ascontiguousarray(w0qkvT[:, c * 768:(c + 1) * 768]))
        b0s = np.asarray(ins["L0_attn_b_qkv"][c * 768:(c + 1) * 768], np.float32)
        cols = [np.ascontiguousarray(
            w1qkv[:, sec * Emb + 4 * c * 128: sec * Emb + 4 * (c + 1) * 128])
            for sec in range(3)]
        w1s = _tile_w(np.concatenate(cols, axis=1))
        w0d_t = _tile_w(np.ascontiguousarray(w0dT[4 * c * 128:4 * (c + 1) * 128, :]))
        w1o_t = _tile_w(np.ascontiguousarray(w1o_full[4 * c * 128:4 * (c + 1) * 128, :]))
        b1s = np.concatenate([
            np.asarray(ins["L1_attn_qkv_b"][sec * Emb + 4 * c * 128: sec * Emb + 4 * (c + 1) * 128], np.float32)
            for sec in range(3)])
        in_maps.append({
            "x0": np.stack([xt[b, i] for (b, i) in GATHER[c]]),
            "eye": eye,
            "r0cos": r0c, "r0sin": r0s, "r1cos": r1c, "r1sin": r1s,
            "m0": L0_PATS, "m1": m1,
            "l0k": l0k_t,
            "l0kb": np.ascontiguousarray(np.asarray(ins["L0_ffn_bias"], np.float32).reshape(-1, 128).T),
            "gT": gT, "lAT": lAT, "lBT": lBT, "lAsT": lAsT, "lBsT": lBsT,
            "w0qkv": w0s,
            "b0qkv": np.ascontiguousarray(b0s.reshape(-1, 128).T),
            "w0d": w0d_t,
            "b0d": np.ascontiguousarray(np.asarray(ins["L0_attn_b_dense"], np.float32).reshape(-1, 128).T),
            "w1qkv": w1s,
            "b1qkv": np.ascontiguousarray(b1s.reshape(-1, 128).T),
            "w1o": w1o_t,
            "b1o": np.ascontiguousarray(np.asarray(ins["L1_attn_out_b"], np.float32).reshape(-1, 128).T),
            "qwg": np.ascontiguousarray(ins["L1_ffn_qweight_gu"]),
            "qzg": np.ascontiguousarray(ins["L1_ffn_qzeros_gu"]),
            "scg": np.ascontiguousarray(ins["L1_ffn_scales_gu"], np.float32),
            "qwd": np.ascontiguousarray(ins["L1_ffn_qweight_dn"]),
            "qzd": np.ascontiguousarray(ins["L1_ffn_qzeros_dn"]),
            "scd": np.ascontiguousarray(ins["L1_ffn_scales_dn"], np.float32),
        })

    nc = build_program()
    with allow_ici_all_to_all():
        res = run_bass_kernel_spmd(nc, in_maps, core_ids=list(range(NCORE)))

    y = np.empty((2, TB, 128, D), dtype=np.float32)
    for c in range(NCORE):
        o = res.results[c]["out"]
        for l, (b, i) in enumerate(GATHER[c]):
            y[b, i] = o[l]
    return y.reshape(2, SEQ, D)
